# version 02
# baseline (speedup 1.0000x reference)
import numpy as np
from contextlib import ExitStack

import concourse.bass as bass
import concourse.mybir as mybir
from concourse.bass_utils import run_bass_kernel_spmd

F32 = mybir.dt.float32
BF16 = mybir.dt.bfloat16
I32 = mybir.dt.int32
AF = mybir.ActivationFunctionType
ALU = mybir.AluOpType
AX = mybir.AxisListType.X
MAGIC = 12582912.0

B, T, C = 4, 4096, 1024
TH = T // 2
CH = C // 2
CG = CH // 128
G8 = C // 128
TCH = 512
NCH = T // TCH
NTC = TH // TCH
I3 = 3072
IG = I3 // 128
QMAX = 32767.0

TILE = 128 * TCH


def build_nc(debug=0):
    nc = bass.Bass("TRN2", target_bir_lowering=False, debug=False)

    xT = nc.dram_tensor("xT", [C, T], F32, kind="ExternalInput").ap()
    xTh = nc.dram_tensor("xTh", [C, TH], F32, kind="ExternalInput").ap()
    wrvkT = nc.dram_tensor("wrvkT", [C, 3 * CH], F32, kind="ExternalInput").ap()
    woutT_d = nc.dram_tensor("woutT_d", [C, C], F32, kind="ExternalInput").ap()
    tdv = nc.dram_tensor("tdv", [128, CG], F32, kind="ExternalInput").ap()
    tfv = nc.dram_tensor("tfv", [128, CG], F32, kind="ExternalInput").ap()
    ident_in = nc.dram_tensor("ident_in", [128, 128], F32, kind="ExternalInput").ap()
    agidx_in = nc.dram_tensor("agidx_in", [1, G8 * NTC], I32, kind="ExternalInput").ap()
    wup = nc.dram_tensor("wup", [I3, C], F32, kind="ExternalInput").ap()
    wupT = nc.dram_tensor("wupT", [C, I3], F32, kind="ExternalInput").ap()
    wgate = nc.dram_tensor("wgate", [I3, C], F32, kind="ExternalInput").ap()
    wgateT = nc.dram_tensor("wgateT", [C, I3], F32, kind="ExternalInput").ap()
    wdown = nc.dram_tensor("wdown", [C, I3], F32, kind="ExternalInput").ap()
    wdownT = nc.dram_tensor("wdownT", [I3, C], F32, kind="ExternalInput").ap()

    out_fm = nc.dram_tensor("out_fm", [C, TH], F32, kind="ExternalOutput").ap()
    dbg_kind = "ExternalOutput" if debug else "Internal"
    dwkv = nc.dram_tensor("dwkv", [CH, T], F32, kind=dbg_kind).ap() if debug else None
    dxln = nc.dram_tensor("dxln", [C, TH], F32, kind=dbg_kind).ap() if debug else None

    ag_in = nc.dram_tensor("ag_in", [CG, NCH, 128, TCH], F32)
    ag_out = nc.dram_tensor("ag_out", [8, CG, NCH, 128, TCH], F32)
    cc0_i = nc.dram_tensor("cc0_i", [1, 128], F32)
    cc0_o = nc.dram_tensor("cc0_o", [1, 128], F32)
    ccg_i = nc.dram_tensor("ccg_i", [1, 128], F32)
    ccg_o = nc.dram_tensor("ccg_o", [1, 128], F32)
    ccu_i = nc.dram_tensor("ccu_i", [2, 128], F32)
    ccu_o = nc.dram_tensor("ccu_o", [2, 128], F32)
    cc3_i = nc.dram_tensor("cc3_i", [1, 128], F32)
    cc3_o = nc.dram_tensor("cc3_o", [1, 128], F32)
    up_d = nc.dram_tensor("up_d", [NTC, IG, 128, TCH], F32)
    gate_d = nc.dram_tensor("gate_d", [NTC, IG, 128, TCH], F32)
    p_d = nc.dram_tensor("p_d", [NTC, IG, 128, TCH], F32)
    p2_d = nc.dram_tensor("p2_d", [NTC, IG, 128, TCH], F32)
    xln_d = nc.dram_tensor("xln_d", [G8, 128, TH], F32)
    mx_d = nc.dram_tensor("mx_d", [1, 128], F32)
    bcast_d = nc.dram_tensor("bcast_d", [1, 16], F32)
    scl_up = nc.dram_tensor("scl_up", [2, 32, 32, I3], F32)
    scl_gate = nc.dram_tensor("scl_gate", [2, 32, 32, I3], F32)
    scl_down = nc.dram_tensor("scl_down", [2, 96, 32, C], F32)
    wqd_hi = nc.dram_tensor("wqd_hi", [G8, IG, 128, 128], BF16)
    wqd_lo = nc.dram_tensor("wqd_lo", [G8, IG, 128, 128], BF16)

    es = ExitStack()

    names = ["qP0", "qXL", "qXH", "qXS", "qWL", "qWH", "qWS", "qTE", "qSE", "qVC", "qRD", "qSL", "qSV", "qAG", "qOR", "qOH", "qOL", "qOM", "qOE", "qSQ", "qLS", "qLB", "qWO1", "qWO2", "qXR", "qTT", "qTS", "qTD", "qNX", "qNO", "qMM2", "qEV2", "qST", "qRB", "qSM", "vself", "sself", "qLAv", "qLAs", "qCNv", "qCNs", "qOR1", "qOH1", "qXLS", "qB0", "qBg", "qBu", "qB3", "qRB0", "qRB1", "qWT0", "qWT1", "qDQ", "qMMg", "qMMu", "qEVg", "qEVu", "qSP20", "qSP21", "qLg0", "qLg1", "qLu0", "qLu1", "qLp0", "qLp1", "qLq0", "qLq1", "qLx20", "qLx21", "qLW0", "qLW1", "qWQS", "qCHg", "qMXg", "qMXu", "qMX2", "qV2", "qDN", "qLxl0", "qLxl1", "qBC0", "qBCg", "qCHa", "qSXg", "qMXt", "qMXt2", "qNUP", "qP2a", "qP2b", "qOE2", "qBCu", "qBC3", "qS0", "qS1", "qS2", "qS3", "qC0", "qCg", "qCu", "qC3", "qCA"]
    SM = {n: es.enter_context(nc.semaphore(n)) for n in names}

    TOT = {n: 0 for n in names}
    CHST = {}

    class ChProxy:
        def __init__(self, ch, ins):
            self.ch, self.ins = ch, ins

        def then_inc(self, semname, k=1):
            TOT[semname] += k
            self.ins.then_inc(SM[semname], k)
            self.ch.pend_inc = (SM[semname], TOT[semname])
            return self

    class Chain:
        def __init__(self, eng, key):
            self.e = eng
            self.key = key
            st = CHST.setdefault(key, {"last": None, "pend": None, "pend_inc": None})
            self.st = st

        @property
        def pend_inc(self):
            return self.st["pend_inc"]

        @pend_inc.setter
        def pend_inc(self, v):
            self.st["pend_inc"] = v

        def _flush(self):
            st = self.st
            if st["pend"] is not None:
                if st["pend_inc"] is None:
                    nm = self.key
                    TOT[nm] += 1
                    st["pend"].then_inc(SM[nm], 1)
                    st["last"] = (SM[nm], TOT[nm])
                else:
                    st["last"] = st["pend_inc"]
                st["pend"] = None
                st["pend_inc"] = None

        def wait_ge(self, sem, val):
            self._flush()
            if val > 0:
                self.e.wait_ge(sem, val)

        def __getattr__(self, m):
            f = getattr(self.e, m)
            if not callable(f):
                return f
            ch = self

            def wrap(*a, **kw):
                ch._flush()
                if ch.st["last"] is not None:
                    ch.e.wait_ge(*ch.st["last"])
                ins = f(*a, **kw)
                ch.st["pend"] = ins
                return ChProxy(ch, ins)
            return wrap

    ident = es.enter_context(nc.sbuf_tensor("ident", [128, 128], F32))
    ones_col = es.enter_context(nc.sbuf_tensor("ones_col", [128, 1], F32))
    ones_row = es.enter_context(nc.sbuf_tensor("ones_row", [1, 128], F32))
    euw = es.enter_context(nc.sbuf_tensor("euw", [128, 2 * CG], F32))
    bc = es.enter_context(nc.sbuf_tensor("bc", [128, 16], F32))
    mxt = es.enter_context(nc.sbuf_tensor("mxt", [128, 8], F32))
    sc1 = es.enter_context(nc.sbuf_tensor("sc1", [1, 16], F32))
    cst = es.enter_context(nc.sbuf_tensor("cst", [1, 8], F32))
    agix = es.enter_context(nc.sbuf_tensor("agix", [1, G8 * NTC], I32))
    sAll = es.enter_context(nc.sbuf_tensor("sAll", [128, 2 * (768 + 768 + 768)], F32))
    stg96 = es.enter_context(nc.sbuf_tensor("stg96", [96, 128], F32))
    lnt = es.enter_context(nc.sbuf_tensor("lnt", [1, 4 * TCH], F32))
    mrow = es.enter_context(nc.sbuf_tensor("mrow", [1, 256], F32))
    pb = [es.enter_context(nc.psum_tensor(f"pb{i}", [128, TCH], F32)) for i in range(8)]

    BC_R1, BC_CG, BC_S3, BC_CUP, BC_C2, BC_C3, BC_S5 = range(7)
    MX_X, MX_UP, MX_P, MX_P2 = range(4)
    SC_S1, SC_S3, SC_K4, SC_S2, SC_S4, SC_K5, SC_S5 = range(7)
    CS_EPS, CS_NHALF, CS_ONE = 0, 1, 2

    SOFF = {"up": 0, "gate": 1536, "down": 3072}
    NBLK = {"up": 32, "gate": 32, "down": 96}
    NOST = {"up": 24, "gate": 24, "down": 8}
    NATW = {"up": wup, "gate": wgate, "down": wdown}
    SCLD = {"up": scl_up, "gate": scl_gate, "down": scl_down}

    with nc.Block() as blk:
        @blk.sync
        def _(sy):
            sy.dma_start(out=ident[:, :], in_=ident_in).then_inc(SM["qP0"], 16)
            sy.dma_start(out=euw[:, 0:CG], in_=tdv).then_inc(SM["qP0"], 16)
            sy.dma_start(out=euw[:, CG:2 * CG], in_=tfv).then_inc(SM["qP0"], 16)
            sy.dma_start(out=agix[:, :], in_=agidx_in).then_inc(SM["qP0"], 16)

        @blk.vector
        def _(v):
            v = Chain(v, "vself")
            v.memset(ones_col[:, :], 1.0)
            v.memset(ones_row[:, :], 1.0)
            v.memset(mxt[:, :], 0.0)
            v.memset(cst[:, CS_EPS:CS_EPS + 1], 1e-5)
            v.memset(cst[:, CS_NHALF:CS_NHALF + 1], -0.5)
            v.memset(cst[:, CS_ONE:CS_ONE + 1], 1.0).then_inc("qCNv", 1)

        @blk.scalar
        def _(s):
            s = Chain(s, "sself")
            s.wait_ge(SM["qP0"], 64)
            s.activation(euw[:, 0:CG], euw[:, 0:CG], AF.Exp)
            s.activation(euw[:, 0:CG], euw[:, 0:CG], AF.Exp, scale=-1.0)
            s.activation(euw[:, CG:2 * CG], euw[:, CG:2 * CG], AF.Exp).then_inc("qCNs", 1)

    esA = ExitStack()
    wrvk_f = esA.enter_context(nc.sbuf_tensor("wrvk_f", [128, 3 * CH], F32))
    wrvk_hi = esA.enter_context(nc.sbuf_tensor("wrvk_hi", [128, G8 * 3 * CH], BF16))
    wrvk_lo = esA.enter_context(nc.sbuf_tensor("wrvk_lo", [128, G8 * 3 * CH], BF16))
    xf = esA.enter_context(nc.sbuf_tensor("xf", [128, G8 * TCH], F32))
    xhi = esA.enter_context(nc.sbuf_tensor("xhi", [128, 2 * G8 * TCH], BF16))
    xlo = esA.enter_context(nc.sbuf_tensor("xlo", [128, 2 * G8 * TCH], BF16))
    vbuf = esA.enter_context(nc.sbuf_tensor("vbuf", [128, CG * TCH], F32))
    ekb = esA.enter_context(nc.sbuf_tensor("ekb", [128, CG * (TCH + 1)], F32))
    ekvb = esA.enter_context(nc.sbuf_tensor("ekvb", [128, CG * (TCH + 1)], F32))
    ab = esA.enter_context(nc.sbuf_tensor("ab", [128, CG * (TCH + 1)], F32))
    bbf = esA.enter_context(nc.sbuf_tensor("bbf", [128, CG * (TCH + 1)], F32))
    numb = esA.enter_context(nc.sbuf_tensor("numb", [128, CG * TCH], F32))
    denb = esA.enter_context(nc.sbuf_tensor("denb", [128, CG * TCH], F32))
    erb = esA.enter_context(nc.sbuf_tensor("erb", [128, CG * TCH], F32))
    wnat = esA.enter_context(nc.sbuf_tensor("wnat", [128, I3], F32))
    sblk = esA.enter_context(nc.sbuf_tensor("sblk", [128, 96], F32))

    def s3d(t, off, strideg, ng, n):
        return bass.AP(t.ap().tensor, off, [[t.shape[1], 128], [strideg, ng], [1, n]])

    strip_sched = [(w, o) for w in ("gate", "up", "down") for o in range(NOST[w])]

    with nc.Block() as blk:
        @blk.sync
        def _(sy):
            for cg in range(G8):
                if cg >= 1:
                    sy.wait_ge(SM["qWS"], cg)
                sy.dma_start(out=wrvk_f[:, :],
                             in_=wrvkT[cg * 128:(cg + 1) * 128, :]).then_inc(SM["qWL"], 16)
            for j in range(NCH):
                if j >= 1:
                    sy.wait_ge(SM["qXS"], j)
                sy.dma_start(
                    out=xf[:, :],
                    in_=bass.AP(xT.tensor, j * TCH,
                                [[T, 128], [128 * T, G8], [1, TCH]])).then_inc(SM["qXL"], 16)
                for k in range(7 * j, min(7 * (j + 1), len(strip_sched))):
                    w, o = strip_sched[k]
                    iw = C if w in ("up", "gate") else I3
                    if k >= 1:
                        sy.wait_ge(SM["qSV"], k)
                    sy.dma_start(out=wnat[:, 0:iw],
                                 in_=NATW[w][o * 128:(o + 1) * 128, :]).then_inc(SM["qSL"], 16)

        @blk.gpsimd
        def _(g):
            for cg in range(G8):
                g.wait_ge(SM["qWL"], 16 * (cg + 1))
                g.dma_start(out=wrvk_hi[:, cg * 3 * CH:(cg + 1) * 3 * CH],
                            in_=wrvk_f[:, :]).then_inc(SM["qWH"], 16)
            for j in range(NCH):
                g.wait_ge(SM["qXL"], 16 * (j + 1))
                if j >= 2:
                    g.wait_ge(SM["qTE"], 12 * (j - 1))
                rx = (j % 2) * 4096
                g.dma_start(out=xhi[:, rx:rx + 4096], in_=xf[:, :]).then_inc(SM["qXH"], 16)
                g.wait_ge(SM["qVC"], j + 1)
                g.dma_start(
                    out=bass.AP(ag_in, j * 128 * TCH,
                                [[TCH, 128], [NCH * 128 * TCH, CG], [1, TCH]]),
                    in_=numb[:, :]).then_inc(SM["qRD"], 16)
                if debug:
                    g.dma_start(
                        out=bass.AP(dwkv.tensor, j * TCH,
                                    [[T, 128], [128 * T, CG], [1, TCH]]),
                        in_=numb[:, :]).then_inc(SM["qRD"], 16)

        @blk.vector
        def _(v):
            v = Chain(v, "vself")
            for cg in range(G8):
                v.wait_ge(SM["qWH"], 16 * (cg + 1))
                v.tensor_sub(wrvk_lo[:, cg * 3 * CH:(cg + 1) * 3 * CH], wrvk_f[:, :],
                             wrvk_hi[:, cg * 3 * CH:(cg + 1) * 3 * CH]).then_inc("qWS", 1)
            v.wait_ge(SM["qCNs"], 1)
            for g in range(CG):
                v.memset(ekvb[:, g * (TCH + 1):g * (TCH + 1) + 1], 0.0)
                v.memset(ekb[:, g * (TCH + 1):g * (TCH + 1) + 1], 0.0)
            for j in range(NCH):
                v.wait_ge(SM["qXH"], 16 * (j + 1))
                rx = (j % 2) * 4096
                v.tensor_sub(xlo[:, rx:rx + 4096], xf[:, :], xhi[:, rx:rx + 4096]).then_inc("qXS", 1)
                v.wait_ge(SM["qSE"], 12 * (j + 1))
                if j >= 1:
                    v.wait_ge(SM["qRD"], 16 * j * (2 if debug else 1))
                    v.tensor_copy(s3d(ekvb, 0, TCH + 1, CG, 1), s3d(ab, TCH, TCH + 1, CG, 1))
                    v.tensor_copy(s3d(ekb, 0, TCH + 1, CG, 1), s3d(bbf, TCH, TCH + 1, CG, 1))
                v.tensor_mul(s3d(ekvb, 1, TCH + 1, CG, TCH), s3d(ekb, 1, TCH + 1, CG, TCH),
                             s3d(vbuf, 0, TCH, CG, TCH))
                for g in range(CG):
                    o = g * (TCH + 1)
                    v.tensor_tensor_scan(ab[:, o:o + TCH + 1],
                                         euw[:, g:g + 1].broadcast_to([128, TCH + 1]),
                                         ekvb[:, o:o + TCH + 1], 0.0, ALU.mult, ALU.add)
                    v.tensor_tensor_scan(bbf[:, o:o + TCH + 1],
                                         euw[:, g:g + 1].broadcast_to([128, TCH + 1]),
                                         ekb[:, o:o + TCH + 1], 0.0, ALU.mult, ALU.add)
                for g in range(CG):
                    o = g * (TCH + 1)
                    q = g * TCH
                    v.scalar_tensor_tensor(numb[:, q:q + TCH], ekvb[:, o + 1:o + TCH + 1],
                                           euw[:, CG + g:CG + g + 1], ab[:, o:o + TCH],
                                           ALU.mult, ALU.add)
                    v.scalar_tensor_tensor(denb[:, q:q + TCH], ekb[:, o + 1:o + TCH + 1],
                                           euw[:, CG + g:CG + g + 1], bbf[:, o:o + TCH],
                                           ALU.mult, ALU.add)
                v.reciprocal(denb[:, :], denb[:, :])
                v.tensor_mul(numb[:, :], numb[:, :], denb[:, :])
                v.tensor_scalar_add(erb[:, :], erb[:, :], 1.0)
                v.reciprocal(erb[:, :], erb[:, :])
                v.tensor_mul(numb[:, :], numb[:, :], erb[:, :]).then_inc("qVC", 1)
                for k in range(7 * j, min(7 * (j + 1), len(strip_sched))):
                    w, o = strip_sched[k]
                    nb = NBLK[w]
                    v.wait_ge(SM["qSL"], 16 * (k + 1))
                    v.reduce_max(
                        bass.AP(sblk.ap().tensor, 0, [[96, 128], [1, nb], [0, 1]]),
                        bass.AP(wnat.ap().tensor, 0, [[I3, 128], [32, nb], [1, 32]]),
                        axis=AX, apply_absolute_value=True)
                    soff = SOFF[w]
                    v.tensor_scalar(sAll[:, soff + 768 + o * nb: soff + 768 + (o + 1) * nb],
                                    sblk[:, 0:nb], 1.0 / 7.0, 1e-12, ALU.mult,
                                    ALU.max).then_inc("qSV", 1)

        @blk.tensor
        def _(t):
            t.wait_ge(SM["qWH"], 16 * G8)
            t.wait_ge(SM["qWS"], G8)
            fills = 0
            for j in range(NCH):
                t.wait_ge(SM["qXS"], j + 1)
                for og in range(12):
                    if fills >= 4:
                        t.wait_ge(SM["qSE"], fills - 3)
                    bank = pb[fills % 4]
                    for kg in range(G8):
                        lh = wrvk_hi[:, kg * 3 * CH + og * 128: kg * 3 * CH + (og + 1) * 128]
                        ll = wrvk_lo[:, kg * 3 * CH + og * 128: kg * 3 * CH + (og + 1) * 128]
                        rx = (j % 2) * 4096
                        rh = xhi[:, rx + kg * TCH:rx + (kg + 1) * TCH]
                        rl = xlo[:, rx + kg * TCH:rx + (kg + 1) * TCH]
                        t.matmul(bank[:, :], lh, rh, start=(kg == 0), stop=False)
                        t.matmul(bank[:, :], lh, rl, start=False, stop=False)
                        mm = t.matmul(bank[:, :], ll, rh, start=False,
                                      stop=(kg == G8 - 1))
                        if kg == G8 - 1:
                            mm.then_inc(SM["qTE"], 1)
                    fills += 1

        @blk.scalar
        def _(s):
            s = Chain(s, "sself")
            fills = 0
            for j in range(NCH):
                if j >= 1:
                    s.wait_ge(SM["qVC"], j)
                for og in range(12):
                    s.wait_ge(SM["qTE"], fills + 1)
                    bank = pb[fills % 4]
                    g4 = og % 4
                    if og < 4:
                        ins = s.activation(erb[:, g4 * TCH:(g4 + 1) * TCH], bank[:, :],
                                           AF.Exp, scale=-1.0)
                    elif og < 8:
                        ins = s.copy(vbuf[:, g4 * TCH:(g4 + 1) * TCH], bank[:, :])
                    else:
                        ins = s.activation(ekb[:, g4 * (TCH + 1) + 1:(g4 + 1) * (TCH + 1)],
                                           bank[:, :], AF.Exp)
                    ins.then_inc("qSE", 1)
                    fills += 1

    esA.close()
    nAG = 16 * NCH * (2 if debug else 1)
    esB = ExitStack()
    xres = esB.enter_context(nc.sbuf_tensor("xres", [128, G8 * TH], F32))
    wo_f = esB.enter_context(nc.sbuf_tensor("wo_f", [128, C], F32))
    wo_hi = esB.enter_context(nc.sbuf_tensor("wo_hi", [128, G8 * C], BF16))
    wo_lo = esB.enter_context(nc.sbuf_tensor("wo_lo", [128, G8 * C], BF16))
    rrf = esB.enter_context(nc.sbuf_tensor("rrf", [128, 2 * G8 * TCH], F32))
    rrh = esB.enter_context(nc.sbuf_tensor("rrh", [128, 2 * G8 * TCH], BF16))
    rrl = esB.enter_context(nc.sbuf_tensor("rrl", [128, 2 * G8 * TCH], BF16))
    sqt = esB.enter_context(nc.sbuf_tensor("sqt", [128, TCH], F32))
    tmx = esB.enter_context(nc.sbuf_tensor("tmx", [128, 1], F32))
    NSCT = 2 * (NOST["up"] + NOST["gate"] + NOST["down"])

    with nc.Block() as blk:
        @blk.gpsimd
        def _(g):
            g.wait_ge(SM["qRD"], nAG)
            g.collective_compute(
                "AllGather", ALU.bypass, replica_groups=[list(range(8))],
                ins=[ag_in.ap().opt()], outs=[ag_out.ap().opt()]).then_inc(SM["qAG"], 1)
            nt = 0
            for w in ("gate", "up", "down"):
                nb = NBLK[w]
                ow = I3 if w in ("up", "gate") else C
                for o in range(NOST[w]):
                    for half in range(2):
                        g.wait_ge(SM["qTS"], nt + 1)
                        g.dma_start(
                            out=bass.AP(SCLD[w], half * nb * 32 * ow + o * 128,
                                        [[32 * ow, nb], [ow, 32], [1, 128]]),
                            in_=bass.AP(stg96.ap().tensor, 0,
                                        [[128, nb], [0, 32], [1, 128]])).then_inc(SM["qTD"], 16)
                        nt += 1
            g.wait_ge(SM["qAG"], 1)
            with g.register("agoff") as roff:
                for tc in range(NTC):
                    rt = (tc % 2) * 4096
                    if tc >= 2:
                        g.wait_ge(SM["qOL"], tc - 1)
                        g.wait_ge(SM["qOM"], (tc - 1) * G8)
                    qor = "qOR" if tc % 2 == 0 else "qOR1"
                    qoh = "qOH" if tc % 2 == 0 else "qOH1"
                    g.reg_load(roff, agix[0:1, tc:tc + 1])
                    src = bass.AP(ag_out, g.snap(roff),
                                  [[TCH, 128], [CG * NCH * TILE, 2], [NCH * TILE, CG], [1, TCH]])
                    g.dma_start(out=rrf[:, rt:rt + 4096], in_=src).then_inc(SM[qor], 16)
                    g.wait_ge(SM[qor], 16 * (tc // 2 + 1))
                    g.dma_start(out=rrh[:, rt:rt + 4096],
                                in_=rrf[:, rt:rt + 4096]).then_inc(SM[qoh], 16)

        @blk.sync
        def _(sy):
            sy.wait_ge(SM["qRD"], nAG)
            sy.wait_ge(SM["qTE"], 12 * NCH)
            sy.wait_ge(SM["qSE"], 12 * NCH)
            sy.wait_ge(SM["qSV"], len(strip_sched))
            for g in range(G8):
                sy.dma_start(out=xres[:, g * TH:(g + 1) * TH],
                             in_=xTh[g * 128:(g + 1) * 128, :]).then_inc(SM["qXR"], 16)
            for cg in range(G8):
                if cg >= 1:
                    sy.wait_ge(SM["qWO2"], cg)
                sy.dma_start(out=wo_f[:, :],
                             in_=woutT_d[cg * 128:(cg + 1) * 128, :]).then_inc(SM["qWO1"], 16)
            sy.wait_ge(SM["qLAv"], 2 * NTC)
            for g in range(G8):
                sy.dma_start(out=bass.AP(xln_d, g * 128 * TH, [[TH, 128], [1, TH]]),
                             in_=xres[:, g * TH:(g + 1) * TH]).then_inc(SM["qXLS"], 16)

        @blk.tensor
        def _(t):
            t.wait_ge(SM["qSV"], len(strip_sched))
            t.wait_ge(SM["qRB"], 1)
            nt = 0
            for w in ("gate", "up", "down"):
                nb = NBLK[w]
                for o in range(NOST[w]):
                    for half in range(2):
                        soff = SOFF[w] + 768 * half + o * nb
                        if nt >= 1:
                            t.wait_ge(SM["qTS"], nt)
                        t.transpose(pb[4][0:nb, 0:128], sAll[:, soff:soff + nb],
                                    ident[:, :]).then_inc(SM["qTT"], 1)
                        nt += 1
            t.wait_ge(SM["qWO2"], G8)
            t.wait_ge(SM["qSE"], 12 * NCH)
            t.wait_ge(SM["qTS"], NSCT)
            for tc in range(NTC):
                t.wait_ge(SM["qOL"], tc + 1)
                for og in range(G8):
                    if tc * G8 + og >= 2:
                        t.wait_ge(SM["qOE"], tc * G8 + og - 1)
                    bank = pb[(tc * G8 + og) % 2]
                    for kg in range(G8):
                        lh = wo_hi[:, kg * C + og * 128:kg * C + (og + 1) * 128]
                        ll = wo_lo[:, kg * C + og * 128:kg * C + (og + 1) * 128]
                        rt = (tc % 2) * 4096
                        rh = rrh[:, rt + kg * TCH:rt + (kg + 1) * TCH]
                        rl = rrl[:, rt + kg * TCH:rt + (kg + 1) * TCH]
                        t.matmul(bank[:, :], lh, rh, start=(kg == 0), stop=False)
                        t.matmul(bank[:, :], lh, rl, start=False, stop=False)
                        mm = t.matmul(bank[:, :], ll, rh, start=False,
                                      stop=(kg == G8 - 1))
                        if kg == G8 - 1:
                            mm.then_inc(SM["qOM"], 1)
                for g in range(G8):
                    t.wait_ge(SM["qOE"], tc * G8 + g + 1)
                    xs = xres[:, g * TH + tc * TCH: g * TH + (tc + 1) * TCH]
                    t.matmul(pb[4][0:1, :], ones_col[:, :], xs,
                             start=(g == 0), stop=(g == G8 - 1))
                    t.wait_ge(SM["qSQ"], tc * G8 + g + 1)
                    t.matmul(pb[5][0:1, :], ones_col[:, :], sqt[:, :],
                             start=(g == 0), stop=(g == G8 - 1)).then_inc(SM["qSM"], 1)
                t.wait_ge(SM["qLB"], tc + 1)
                t.matmul(pb[6][:, :], ones_row[:, :], lnt[0:1, 2 * TCH:3 * TCH],
                         start=True, stop=True)
                t.matmul(pb[7][:, :], ones_row[:, :], lnt[0:1, 3 * TCH:4 * TCH],
                         start=True, stop=True).then_inc(SM["qLS"], 1)

        @blk.vector
        def _(v):
            v = Chain(v, "vself")
            v.wait_ge(SM["qSV"], len(strip_sched))
            for w in ("up", "gate", "down"):
                so = SOFF[w]
                v.reciprocal(sAll[:, so:so + 768], sAll[:, so + 768:so + 1536])
            v.engine_nop().then_inc("qRB", 1)
            for cg in range(G8):
                v.wait_ge(SM["qWO1"], 16 * (cg + 1))
                v.tensor_copy(wo_hi[:, cg * C:(cg + 1) * C], wo_f[:, :])
                v.tensor_sub(wo_lo[:, cg * C:(cg + 1) * C], wo_f[:, :],
                             wo_hi[:, cg * C:(cg + 1) * C]).then_inc("qWO2", 1)
            v.wait_ge(SM["qXR"], 16 * G8)
            for tc in range(NTC):
                rt = (tc % 2) * 4096
                v.wait_ge(SM["qOH" if tc % 2 == 0 else "qOH1"], 16 * (tc // 2 + 1))
                v.tensor_sub(rrl[:, rt:rt + 4096], rrf[:, rt:rt + 4096],
                             rrh[:, rt:rt + 4096]).then_inc("qOL", 1)
                for og in range(G8):
                    v.wait_ge(SM["qOM"], tc * G8 + og + 1)
                    bank = pb[(tc * G8 + og) % 2]
                    xs = xres[:, og * TH + tc * TCH: og * TH + (tc + 1) * TCH]
                    v.tensor_add(xs, bank[:, :], xs).then_inc("qOE", 1)
                v.wait_ge(SM["qSM"], G8 * (tc + 1))
                v.tensor_scalar_mul(lnt[0:1, 0:TCH], pb[4][0:1, :], 1.0 / C)
                v.tensor_scalar_mul(lnt[0:1, TCH:2 * TCH], pb[5][0:1, :], 1.0 / C)
                v.tensor_mul(lnt[0:1, 2 * TCH:3 * TCH], lnt[0:1, 0:TCH], lnt[0:1, 0:TCH])
                v.tensor_sub(lnt[0:1, TCH:2 * TCH], lnt[0:1, TCH:2 * TCH],
                             lnt[0:1, 2 * TCH:3 * TCH]).then_inc("qLAv", 1)
                v.wait_ge(SM["qLAs"], tc + 1)
                v.tensor_mul(lnt[0:1, 3 * TCH:4 * TCH], lnt[0:1, 0:TCH],
                             lnt[0:1, 2 * TCH:3 * TCH]).then_inc("qLB", 1)
                v.wait_ge(SM["qLS"], tc + 1)
                for g in range(G8):
                    xs = xres[:, g * TH + tc * TCH: g * TH + (tc + 1) * TCH]
                    v.tensor_mul(xs, xs, pb[6][:, :])
                    v.tensor_sub(xs, xs, pb[7][:, :])
                    v.reduce_max(tmx[:, :], xs, axis=AX, apply_absolute_value=True)
                    v.tensor_max(mxt[:, MX_X:MX_X + 1], mxt[:, MX_X:MX_X + 1], tmx[:, :])
                v.engine_nop().then_inc("qLAv", 1)

        @blk.scalar
        def _(s):
            s = Chain(s, "sself")
            nt = 0
            for w in ("gate", "up", "down"):
                nb = NBLK[w]
                for o in range(NOST[w]):
                    for half in range(2):
                        s.wait_ge(SM["qTT"], nt + 1)
                        if nt >= 1:
                            s.wait_ge(SM["qTD"], 16 * nt)
                        s.copy(stg96[0:nb, :], pb[4][0:nb, 0:128]).then_inc("qTS", 1)
                        nt += 1
            for tc in range(NTC):
                for g in range(G8):
                    idx = tc * G8 + g
                    s.wait_ge(SM["qOE"], idx + 1)
                    if idx >= 1:
                        s.wait_ge(SM["qSM"], idx)
                    xs = xres[:, g * TH + tc * TCH: g * TH + (tc + 1) * TCH]
                    s.activation(sqt[:, :], xs, AF.Square).then_inc("qSQ", 1)
                s.wait_ge(SM["qLAv"], 2 * tc + 1)
                s.activation(lnt[0:1, 2 * TCH:3 * TCH], lnt[0:1, TCH:2 * TCH], AF.Ln,
                             bias=cst[0:1, CS_EPS:CS_EPS + 1])
                s.activation(lnt[0:1, 2 * TCH:3 * TCH], lnt[0:1, 2 * TCH:3 * TCH], AF.Exp,
                             scale=cst[0:1, CS_NHALF:CS_NHALF + 1]).then_inc("qLAs", 1)

    if debug:
        with nc.Block() as blk:
            @blk.sync
            def _(sy):
                sy.wait_ge(SM["qLAv"], 2 * NTC)
                for g in range(G8):
                    sy.dma_start(out=bass.AP(dxln.tensor, g * 128 * TH, [[TH, 128], [1, TH]]),
                                 in_=xres[:, g * TH:(g + 1) * TH]).then_inc(SM["qST"], 16)
                sy.wait_ge(SM["qST"], 16 * G8)

    esB.close()
    ST0 = 16 * G8 if debug else 0

    esC1 = ExitStack()
    nxa = esC1.enter_context(nc.sbuf_tensor("nxa", [128, G8 * TH], BF16))
    nxb = esC1.enter_context(nc.sbuf_tensor("nxb", [128, G8 * TH], BF16))
    xlb = esC1.enter_context(nc.sbuf_tensor("xlb", [128, 2 * TH], F32))
    nxf = esC1.enter_context(nc.sbuf_tensor("nxf", [128, TH], F32))
    wqf = esC1.enter_context(nc.sbuf_tensor("wqf", [128, 2 * 1024], F32))
    rsb = esC1.enter_context(nc.sbuf_tensor("rsb", [128, 2 * 1024], F32))
    ssb = esC1.enter_context(nc.sbuf_tensor("ssb", [128, 2 * 1024], F32))
    wqv = esC1.enter_context(nc.sbuf_tensor("wqv", [128, 2 * 1024], F32))
    wqh = esC1.enter_context(nc.sbuf_tensor("wqh", [128, 2 * 1024], BF16))
    wql = esC1.enter_context(nc.sbuf_tensor("wql", [128, 2 * 1024], BF16))
    stg = esC1.enter_context(nc.sbuf_tensor("stg", [128, 4 * TCH], F32))
    gbuf = esC1.enter_context(nc.sbuf_tensor("gbuf", [128, 2 * 1024], F32))
    ngb = esC1.enter_context(nc.sbuf_tensor("ngb", [128, 2 * 1024], F32))
    egb = esC1.enter_context(nc.sbuf_tensor("egb", [128, 1024], F32))
    ppb = esC1.enter_context(nc.sbuf_tensor("ppb", [128, 2 * 1024], F32))
    tmx2 = esC1.enter_context(nc.sbuf_tensor("tmx2", [128, 1], F32))

    cseq = [("gate", o, 0) for o in range(24)]
    for og in range(24):
        cseq.append(("up", og, 0))
        cseq.append(("down", og // 3, og % 3))
    WT_D = {"up": wupT, "gate": wgateT, "down": wdownT}
    WT_O = {"up": I3, "gate": I3, "down": C}
    MX_G = 4

    def wt_in_ap(w, og, kc):
        O = WT_O[w]
        return bass.AP(WT_D[w].tensor, kc * 8 * 128 * O + og * 128,
                       [[O, 128], [128 * O, 8], [1, 128]])

    def scl_in_ap(w, og, kc, half):
        nb = NBLK[w]
        O = WT_O[w]
        return bass.AP(SCLD[w], half * nb * 32 * O + kc * 32 * 32 * O + og * 128,
                       [[O, 128], [4 * 32 * O, 8], [1, 128]])

    def mm_piece_pos(ph, og):
        return og if ph == 0 else 24 + 2 * og

    with nc.Block() as blk:
        @blk.gpsimd
        def _(g):
            g.wait_ge(SM["qMXg"], 1)
            g.dma_start(out=bass.AP(cc0_i, 0, [[1, 1], [1, 128]]),
                        in_=mxt[:, MX_X:MX_X + 1]).then_inc(SM["qB0"], 16)
            g.wait_ge(SM["qB0"], 16)
            g.collective_compute("AllReduce", ALU.max, replica_groups=[list(range(8))],
                                 ins=[cc0_i.ap().opt()], outs=[cc0_o.ap().opt()]).then_inc(SM["qC0"], 1)
            g.wait_ge(SM["qC0"], 1)
            g.dma_start(out=mrow[0:1, 0:128], in_=cc0_o.ap()).then_inc(SM["qB0"], 16)
            g.wait_ge(SM["qMXg"], 2)
            g.dma_start(out=bass.AP(bcast_d, 0, [[1, 1], [1, 1]]),
                        in_=sc1[0:1, 2:3]).then_inc(SM["qB0"], 16)
            g.wait_ge(SM["qB0"], 48)
            g.dma_start(out=bc[:, BC_R1:BC_R1 + 1],
                        in_=bass.AP(bcast_d, 0, [[0, 128], [1, 1]])).then_inc(SM["qBC0"], 16)
            for k in range(24):
                r = k % 2
                if k >= 2:
                    g.wait_ge(SM["qDQ"], k - 1)
                w, og, kc = cseq[k]
                g.wait_ge(SM["qTD"], 16 * 2 * (og + 1))
                qrb = "qRB0" if r == 0 else "qRB1"
                g.dma_start(out=rsb[:, r * 1024:(r + 1) * 1024],
                            in_=scl_in_ap(w, og, kc, 0)).then_inc(SM[qrb], 16)
                g.dma_start(out=ssb[:, r * 1024:(r + 1) * 1024],
                            in_=scl_in_ap(w, og, kc, 1)).then_inc(SM[qrb], 16)
                for q in range(4):
                    kk = (k - 1) * 4 + q
                    if kk >= 0:
                        g.wait_ge(SM["qEVg"], kk + 1)
                        og2, tc = kk // 4, kk % 4
                        g.dma_start(
                            out=bass.AP(gate_d, (tc * IG + og2) * TILE, [[TCH, 128], [1, TCH]]),
                            in_=stg[:, (kk % 4) * TCH:(kk % 4 + 1) * TCH]).then_inc(SM["qS%d" % (kk % 4)], 16)
            for q in range(4):
                kk = 23 * 4 + q
                g.wait_ge(SM["qEVg"], kk + 1)
                og2, tc = kk // 4, kk % 4
                g.dma_start(
                    out=bass.AP(gate_d, (tc * IG + og2) * TILE, [[TCH, 128], [1, TCH]]),
                    in_=stg[:, (kk % 4) * TCH:(kk % 4 + 1) * TCH]).then_inc(SM["qS%d" % (kk % 4)], 16)
            g.wait_ge(SM["qMXg"], 3)
            g.dma_start(out=bass.AP(ccg_i, 0, [[1, 1], [1, 128]]),
                        in_=mxt[:, MX_G:MX_G + 1]).then_inc(SM["qBg"], 16)
            g.wait_ge(SM["qBg"], 16)
            g.collective_compute("AllReduce", ALU.max, replica_groups=[list(range(8))],
                                 ins=[ccg_i.ap().opt()], outs=[ccg_o.ap().opt()]).then_inc(SM["qCg"], 1)
            g.wait_ge(SM["qCg"], 1)
            g.dma_start(out=mrow[0:1, 128:256], in_=ccg_o.ap()).then_inc(SM["qBg"], 16)
            g.wait_ge(SM["qMXg"], 4)
            g.dma_start(out=bass.AP(bcast_d, 1, [[1, 1], [1, 2]]),
                        in_=sc1[0:1, 5:7]).then_inc(SM["qBg"], 16)
            g.wait_ge(SM["qBg"], 48)
            g.dma_start(out=bc[:, BC_CG:BC_CG + 1],
                        in_=bass.AP(bcast_d, 1, [[0, 128], [1, 1]])).then_inc(SM["qBCg"], 16)
            g.dma_start(out=bc[:, BC_S3:BC_S3 + 1],
                        in_=bass.AP(bcast_d, 2, [[0, 128], [1, 1]])).then_inc(SM["qBCg"], 16)
            for og in range(24):
                for pk in (24 + 2 * og, 25 + 2 * og):
                    r = pk % 2
                    if pk >= 2:
                        g.wait_ge(SM["qDQ"], pk - 1)
                    w, o2, kc = cseq[pk]
                    if w == "up":
                        g.wait_ge(SM["qTD"], 16 * (48 + 2 * (o2 + 1)))
                    else:
                        g.wait_ge(SM["qTD"], 16 * (96 + 2 * (o2 + 1)))
                    qrb = "qRB0" if r == 0 else "qRB1"
                    g.dma_start(out=rsb[:, r * 1024:(r + 1) * 1024],
                                in_=scl_in_ap(w, o2, kc, 0)).then_inc(SM[qrb], 16)
                    g.dma_start(out=ssb[:, r * 1024:(r + 1) * 1024],
                                in_=scl_in_ap(w, o2, kc, 1)).then_inc(SM[qrb], 16)
                for tc in range(4):
                    kk = og * 4 + tc
                    g.wait_ge(SM["qEVu"], kk + 1)
                    g.dma_start(
                        out=bass.AP(up_d, (tc * IG + og) * TILE, [[TCH, 128], [1, TCH]]),
                        in_=stg[:, (kk % 4) * TCH:(kk % 4 + 1) * TCH]).then_inc(SM["qS%d" % (kk % 4)], 16)
                for q in range(2):
                    kk = og * 2 + q
                    g.wait_ge(SM["qCHg"], kk + 1)
                    tck, i2k = kk // 12, kk % 12
                    g.dma_start(
                        out=bass.AP(p_d, (tck * IG + 2 * i2k) * TILE,
                                    [[TCH, 128], [TILE, 2], [1, TCH]]),
                        in_=ppb[:, (kk % 2) * 1024:(kk % 2 + 1) * 1024]
                        ).then_inc(SM["qSP20" if kk % 2 == 0 else "qSP21"], 16)
                dpos = 25 + 2 * og
                d = og
                o3, kc3 = d // 3, d % 3
                g.wait_ge(SM["qDQ"], dpos + 1)
                g.dma_start(
                    out=bass.AP(wqd_hi, (kc3 * 8 * G8 + o3) * 16384,
                                [[128, 128], [G8 * 16384, 8], [1, 128]]),
                    in_=wqh[:, (dpos % 2) * 1024:(dpos % 2 + 1) * 1024]).then_inc(SM["qWQS"], 16)
                g.dma_start(
                    out=bass.AP(wqd_lo, (kc3 * 8 * G8 + o3) * 16384,
                                [[128, 128], [G8 * 16384, 8], [1, 128]]),
                    in_=wql[:, (dpos % 2) * 1024:(dpos % 2 + 1) * 1024]).then_inc(SM["qWQS"], 16)
            g.wait_ge(SM["qMXu"], 1)
            g.dma_start(out=bass.AP(ccu_i, 0, [[1, 128], [128, 2]]),
                        in_=mxt[:, MX_UP:MX_UP + 2],
                        allow_slow_non_contiguous=True).then_inc(SM["qBu"], 16)
            g.wait_ge(SM["qBu"], 16)
            g.collective_compute("AllReduce", ALU.max, replica_groups=[list(range(8))],
                                 ins=[ccu_i.ap().opt()], outs=[ccu_o.ap().opt()]).then_inc(SM["qCu"], 1)
            g.wait_ge(SM["qCu"], 1)
            g.dma_start(out=mrow[0:1, 0:256], in_=ccu_o.ap()).then_inc(SM["qBu"], 16)
            g.wait_ge(SM["qMXu"], 2)
            g.dma_start(out=bass.AP(bcast_d, 3, [[1, 1], [1, 2]]),
                        in_=sc1[0:1, 11:13]).then_inc(SM["qBu"], 16)
            g.wait_ge(SM["qBu"], 48)
            g.dma_start(out=bc[:, BC_CUP:BC_CUP + 1],
                        in_=bass.AP(bcast_d, 3, [[0, 128], [1, 1]])).then_inc(SM["qBCu"], 16)
            g.dma_start(out=bc[:, BC_C2:BC_C2 + 1],
                        in_=bass.AP(bcast_d, 4, [[0, 128], [1, 1]])).then_inc(SM["qBCu"], 16)

        @blk.sync
        def _(sy):
            sy.wait_ge(SM["qXLS"], 16 * G8)
            for g in range(G8):
                if g >= 2:
                    sy.wait_ge(SM["qNX"], g - 1)
                sy.dma_start(out=xlb[:, (g % 2) * TH:(g % 2 + 1) * TH],
                             in_=bass.AP(xln_d, g * 128 * TH, [[TH, 128], [1, TH]])
                             ).then_inc(SM["qLxl0" if g % 2 == 0 else "qLxl1"], 16)
            for k in range(24):
                r = k % 2
                if k >= 2:
                    sy.wait_ge(SM["qDQ"], k - 1)
                sy.dma_start(out=wqf[:, r * 1024:(r + 1) * 1024],
                             in_=wt_in_ap(*cseq[k])).then_inc(SM["qWT0" if r == 0 else "qWT1"], 16)
            for og in range(24):
                for pk in (24 + 2 * og, 25 + 2 * og):
                    r = pk % 2
                    sy.wait_ge(SM["qDQ"], pk - 1)
                    sy.dma_start(out=wqf[:, r * 1024:(r + 1) * 1024],
                                 in_=wt_in_ap(*cseq[pk])).then_inc(SM["qWT0" if r == 0 else "qWT1"], 16)
                for q in range(2):
                    kk = og * 2 + q
                    rr = kk % 2
                    if kk == 0:
                        for j4 in range(4):
                            sy.wait_ge(SM["qS%d" % j4], 16 * 24)
                    if kk >= 2:
                        sy.wait_ge(SM["qCHa"], kk - 1)
                    tck, i2k = kk // 12, kk % 12
                    sy.dma_start(out=gbuf[:, rr * 1024:(rr + 1) * 1024],
                                 in_=bass.AP(gate_d, (tck * IG + 2 * i2k) * TILE,
                                             [[TCH, 128], [TILE, 2], [1, TCH]])
                                 ).then_inc(SM["qLg0" if rr == 0 else "qLg1"], 16)

        @blk.vector
        def _(v):
            v = Chain(v, "vself")
            v.memset(mxt[:, MX_G:MX_G + 1], 0.0)
            v.memset(mxt[:, MX_UP:MX_UP + 2], 0.0).then_inc("qMXg", 1)
            v.wait_ge(SM["qB0"], 32)
            v.reduce_max(sc1[0:1, 1:2], mrow[0:1, 0:128], axis=AX)
            v.tensor_scalar(sc1[0:1, 14:15], sc1[0:1, 1:2], 1.0 / QMAX, 1e-12, ALU.mult, ALU.max)
            v.reciprocal(sc1[0:1, 2:3], sc1[0:1, 14:15]).then_inc("qMXg", 1)
            v.wait_ge(SM["qBC0"], 16)
            for g in range(G8):
                r = g % 2
                v.wait_ge(SM["qLxl0" if r == 0 else "qLxl1"], 16 * (g // 2 + 1))
                xs = xlb[:, r * TH:(r + 1) * TH]
                v.tensor_scalar(nxf[:, :], xs, bc[:, BC_R1:BC_R1 + 1], MAGIC, ALU.mult, ALU.add)
                v.tensor_scalar(nxf[:, :], nxf[:, :], MAGIC, None, ALU.subtract)
                v.tensor_scalar(xs, nxf[:, :], 1.0 / 256.0, MAGIC, ALU.mult, ALU.add)
                v.tensor_scalar(nxa[:, g * TH:(g + 1) * TH], xs, MAGIC, 256.0,
                                ALU.subtract, ALU.mult)
                v.scalar_tensor_tensor(nxb[:, g * TH:(g + 1) * TH], nxa[:, g * TH:(g + 1) * TH],
                                       -1.0, nxf[:, :], ALU.mult, ALU.add).then_inc("qNX", 1)
            for k in range(24):
                r = k % 2
                ra, rb = r * 1024, (r + 1) * 1024
                v.wait_ge(SM["qWT0" if r == 0 else "qWT1"], 16 * (k // 2 + 1))
                v.wait_ge(SM["qRB0" if r == 0 else "qRB1"], 32 * (k // 2 + 1))
                v.tensor_mul(wqv[:, ra:rb], wqf[:, ra:rb], rsb[:, ra:rb])
                v.tensor_scalar(wqv[:, ra:rb], wqv[:, ra:rb], MAGIC, MAGIC, ALU.add, ALU.subtract)
                v.tensor_mul(wqv[:, ra:rb], wqv[:, ra:rb], ssb[:, ra:rb]).then_inc("qDN", 1)
                v.wait_ge(SM["qEV2"], k + 1)
                v.tensor_sub(wql[:, ra:rb], wqv[:, ra:rb], wqh[:, ra:rb]).then_inc("qDQ", 1)
                for q in range(4):
                    kk = k * 4 + q
                    v.wait_ge(SM["qMMg"], kk + 1)
                    v.wait_ge(SM["qEVg"], kk + 1)
                    v.reduce_max(tmx2[:, :], pb[kk % 2][:, :], axis=AX, apply_absolute_value=True)
                    v.tensor_max(mxt[:, MX_G:MX_G + 1], mxt[:, MX_G:MX_G + 1],
                                 tmx2[:, :]).then_inc("qMXt", 1)
            v.engine_nop().then_inc("qMXg", 1)
            v.wait_ge(SM["qBg"], 32)
            v.reduce_max(sc1[0:1, 4:5], mrow[0:1, 128:256], axis=AX)
            v.tensor_mul(sc1[0:1, 4:5], sc1[0:1, 4:5], sc1[0:1, 14:15])
            v.tensor_scalar(sc1[0:1, 15:16], sc1[0:1, 4:5], 1.0 / QMAX, 1e-12, ALU.mult, ALU.max)
            v.reciprocal(sc1[0:1, 5:6], sc1[0:1, 15:16])
            v.tensor_mul(sc1[0:1, 5:6], sc1[0:1, 5:6], sc1[0:1, 14:15])
            v.tensor_scalar(sc1[0:1, 6:7], sc1[0:1, 15:16], -1.0, None, ALU.mult).then_inc("qMXg", 1)
            v.tensor_scalar(sc1[0:1, 13:14], sc1[0:1, 15:16], 1.0 / 65536.0, None, ALU.mult)
            v.wait_ge(SM["qBCg"], 32)
            for og in range(24):
                for pk in (24 + 2 * og, 25 + 2 * og):
                    r = pk % 2
                    ra, rb = r * 1024, (r + 1) * 1024
                    v.wait_ge(SM["qWT0" if r == 0 else "qWT1"], 16 * (pk // 2 + 1))
                    v.wait_ge(SM["qRB0" if r == 0 else "qRB1"], 32 * (pk // 2 + 1))
                    v.tensor_mul(wqv[:, ra:rb], wqf[:, ra:rb], rsb[:, ra:rb])
                    v.tensor_scalar(wqv[:, ra:rb], wqv[:, ra:rb], MAGIC, MAGIC, ALU.add, ALU.subtract)
                    v.tensor_mul(wqv[:, ra:rb], wqv[:, ra:rb], ssb[:, ra:rb]).then_inc("qDN", 1)
                    v.wait_ge(SM["qEV2"], pk + 1)
                    v.tensor_sub(wql[:, ra:rb], wqv[:, ra:rb], wqh[:, ra:rb]).then_inc("qDQ", 1)
                for q in range(2):
                    kk = og * 2 + q
                    rr = kk % 2
                    ra, rb = rr * 1024, (rr + 1) * 1024
                    v.wait_ge(SM["qLg0" if rr == 0 else "qLg1"], 16 * (kk // 2 + 1))
                    v.tensor_scalar(ngb[:, ra:rb], gbuf[:, ra:rb], bc[:, BC_CG:BC_CG + 1],
                                    MAGIC, ALU.mult, ALU.add)
                    v.tensor_scalar(ngb[:, ra:rb], ngb[:, ra:rb], MAGIC, None,
                                    ALU.subtract).then_inc("qCHa", 1)
                    v.wait_ge(SM["qSXg"], kk + 1)
                    v.tensor_scalar_add(egb[:, :], egb[:, :], 1.0)
                    v.reciprocal(egb[:, :], egb[:, :])
                    v.tensor_scalar(egb[:, :], egb[:, :], 65536.0, MAGIC, ALU.mult, ALU.add)
                    v.tensor_scalar(egb[:, :], egb[:, :], MAGIC, 65535.0, ALU.subtract, ALU.min)
                    if kk >= 2:
                        v.wait_ge(SM["qSP20" if kk % 2 == 0 else "qSP21"], 16 * (kk // 2))
                    v.tensor_mul(ppb[:, ra:rb], ngb[:, ra:rb], egb[:, :])
                    v.reduce_max(tmx2[:, :], ppb[:, ra:rb], axis=AX, apply_absolute_value=True)
                    v.tensor_max(mxt[:, MX_P:MX_P + 1], mxt[:, MX_P:MX_P + 1],
                                 tmx2[:, :]).then_inc("qCHg", 1)
                for q in range(4):
                    kk = og * 4 + q
                    v.wait_ge(SM["qMMu"], kk + 1)
                    v.wait_ge(SM["qEVu"], kk + 1)
                    v.reduce_max(tmx2[:, :], pb[kk % 2][:, :], axis=AX, apply_absolute_value=True)
                    v.tensor_max(mxt[:, MX_UP:MX_UP + 1], mxt[:, MX_UP:MX_UP + 1],
                                 tmx2[:, :]).then_inc("qMXt2", 1)
            v.engine_nop().then_inc("qMXu", 1)
            v.wait_ge(SM["qBu"], 32)
            v.reduce_max(sc1[0:1, 9:10], mrow[0:1, 0:128], axis=AX)
            v.reduce_max(sc1[0:1, 10:11], mrow[0:1, 128:256], axis=AX)
            v.tensor_mul(sc1[0:1, 9:10], sc1[0:1, 9:10], sc1[0:1, 14:15])
            v.tensor_scalar(sc1[0:1, 7:8], sc1[0:1, 9:10], 1.0 / QMAX, 1e-12, ALU.mult, ALU.max)
            v.reciprocal(sc1[0:1, 11:12], sc1[0:1, 7:8])
            v.tensor_mul(sc1[0:1, 11:12], sc1[0:1, 11:12], sc1[0:1, 14:15])
            v.tensor_mul(sc1[0:1, 10:11], sc1[0:1, 10:11], sc1[0:1, 13:14])
            v.tensor_scalar(sc1[0:1, 8:9], sc1[0:1, 10:11], 1.0 / QMAX, 1e-12, ALU.mult, ALU.max)
            v.reciprocal(sc1[0:1, 12:13], sc1[0:1, 8:9])
            v.tensor_mul(sc1[0:1, 12:13], sc1[0:1, 12:13], sc1[0:1, 13:14])
            v.tensor_mul(sc1[0:1, 13:14], sc1[0:1, 8:9], sc1[0:1, 7:8]).then_inc("qMXu", 1)

        @blk.scalar
        def _(s):
            s = Chain(s, "sself")
            for k in range(24):
                r = k % 2
                ra, rb = r * 1024, (r + 1) * 1024
                for q in range(4):
                    kk = (k - 1) * 4 + q
                    if kk >= 0:
                        s.wait_ge(SM["qMMg"], kk + 1)
                        jj = kk % 4
                        if kk >= 4:
                            s.wait_ge(SM["qS%d" % jj], 16 * (kk // 4))
                        s.copy(stg[:, jj * TCH:(jj + 1) * TCH], pb[kk % 2][:, :]).then_inc("qEVg", 1)
                s.wait_ge(SM["qDN"], k + 1)
                if k >= 2:
                    s.wait_ge(SM["qMMg"], 4 * (k - 1))
                s.copy(wqh[:, ra:rb], wqv[:, ra:rb]).then_inc("qEV2", 1)
            for q in range(4):
                kk = 23 * 4 + q
                s.wait_ge(SM["qMMg"], kk + 1)
                jj = kk % 4
                s.wait_ge(SM["qS%d" % jj], 16 * (kk // 4))
                s.copy(stg[:, jj * TCH:(jj + 1) * TCH], pb[kk % 2][:, :]).then_inc("qEVg", 1)
            for og in range(24):
                for q in range(4):
                    kk = (og - 1) * 4 + q
                    if kk >= 0:
                        s.wait_ge(SM["qMMu"], kk + 1)
                        jj = kk % 4
                        s.wait_ge(SM["qS%d" % jj], 16 * (24 + kk // 4))
                        s.copy(stg[:, jj * TCH:(jj + 1) * TCH], pb[kk % 2][:, :]).then_inc("qEVu", 1)
                for pk in (24 + 2 * og, 25 + 2 * og):
                    r = pk % 2
                    ra, rb = r * 1024, (r + 1) * 1024
                    s.wait_ge(SM["qDN"], pk + 1)
                    prev = pk - 2
                    if prev >= 24:
                        if cseq[prev][0] == "up":
                            s.wait_ge(SM["qMMu"], 4 * ((prev - 24) // 2 + 1))
                        else:
                            s.wait_ge(SM["qWQS"], 32 * ((prev - 25) // 2 + 1))
                    elif prev >= 0:
                        s.wait_ge(SM["qMMg"], 4 * 24)
                    s.copy(wqh[:, ra:rb], wqv[:, ra:rb]).then_inc("qEV2", 1)
                for q in range(2):
                    kk = og * 2 + q
                    rr = kk % 2
                    s.wait_ge(SM["qCHa"], kk + 1)
                    if kk >= 1:
                        s.wait_ge(SM["qCHg"], kk)
                    s.wait_ge(SM["qBCg"], 32)
                    s.activation(egb[:, :], ngb[:, rr * 1024:(rr + 1) * 1024], AF.Exp,
                                 scale=bc[:, BC_S3:BC_S3 + 1]).then_inc("qSXg", 1)
            for q in range(4):
                kk = 23 * 4 + q
                s.wait_ge(SM["qMMu"], kk + 1)
                jj = kk % 4
                s.wait_ge(SM["qS%d" % jj], 16 * (24 + kk // 4))
                s.copy(stg[:, jj * TCH:(jj + 1) * TCH], pb[kk % 2][:, :]).then_inc("qEVu", 1)

        @blk.tensor
        def _(t):
            t.wait_ge(SM["qNX"], G8)
            for ph, qmm in ((0, "qMMg"), (1, "qMMu")):
                for og in range(24):
                    pos = mm_piece_pos(ph, og)
                    r = pos % 2
                    t.wait_ge(SM["qDQ"], pos + 1)
                    for tc in range(4):
                        kk = og * 4 + tc
                        f = ph * 96 + kk
                        if f >= 2:
                            p = f - 2
                            if p < 96:
                                t.wait_ge(SM["qEVg"], p + 1)
                                t.wait_ge(SM["qMXt"], p + 1)
                            else:
                                t.wait_ge(SM["qEVu"], p - 96 + 1)
                                t.wait_ge(SM["qMXt2"], p - 96 + 1)
                        bank = pb[kk % 2]
                        for kg in range(G8):
                            lh = wqh[:, r * 1024 + kg * 128:r * 1024 + (kg + 1) * 128]
                            ll = wql[:, r * 1024 + kg * 128:r * 1024 + (kg + 1) * 128]
                            rh = nxa[:, kg * TH + tc * TCH:kg * TH + (tc + 1) * TCH]
                            rl = nxb[:, kg * TH + tc * TCH:kg * TH + (tc + 1) * TCH]
                            t.matmul(bank[:, :], lh, rh, start=(kg == 0), stop=False)
                            t.matmul(bank[:, :], lh, rl, start=False, stop=False)
                            mm = t.matmul(bank[:, :], ll, rh, start=False, stop=(kg == G8 - 1))
                            if kg == G8 - 1:
                                mm.then_inc(SM[qmm], 1)
    esC1.close()

    esC2 = ExitStack()
    ub = esC2.enter_context(nc.sbuf_tensor("ub", [128, 2 * 1024], F32))
    pbf = esC2.enter_context(nc.sbuf_tensor("pbf", [128, 2 * 1024], F32))
    nupb = esC2.enter_context(nc.sbuf_tensor("nupb", [128, 1024], F32))
    p2b = esC2.enter_context(nc.sbuf_tensor("p2b", [128, 2 * 1024], F32))
    cstP = esC2.enter_context(nc.sbuf_tensor("cstP", [128, 4], F32))
    tmx3 = esC2.enter_context(nc.sbuf_tensor("tmx3", [128, 1], F32))
    MX_P2 = MX_P + 1

    with nc.Block() as blk:
        @blk.sync
        def _(sy):
            for j4 in range(4):
                sy.wait_ge(SM["qS%d" % j4], 16 * 48)
            sy.wait_ge(SM["qSP20"], 16 * 24)
            sy.wait_ge(SM["qSP21"], 16 * 24)
            for k in range(48):
                r = k % 2
                tc, i2 = k // 12, k % 12
                if k >= 2:
                    sy.wait_ge(SM["qV2"], k - 1)
                sy.dma_start(out=ub[:, r * 1024:(r + 1) * 1024],
                             in_=bass.AP(up_d, (tc * IG + 2 * i2) * TILE,
                                         [[TCH, 128], [TILE, 2], [1, TCH]])
                             ).then_inc(SM["qLu0" if r == 0 else "qLu1"], 16)
                sy.dma_start(out=pbf[:, r * 1024:(r + 1) * 1024],
                             in_=bass.AP(p_d, (tc * IG + 2 * i2) * TILE,
                                         [[TCH, 128], [TILE, 2], [1, TCH]])
                             ).then_inc(SM["qLp0" if r == 0 else "qLp1"], 16)
                if k >= 1:
                    kk = k - 1
                    sy.wait_ge(SM["qV2"], kk + 1)
                    tck, i2k = kk // 12, kk % 12
                    sy.dma_start(
                        out=bass.AP(p2_d, (tck * IG + 2 * i2k) * TILE,
                                    [[TCH, 128], [TILE, 2], [1, TCH]]),
                        in_=p2b[:, (kk % 2) * 1024:(kk % 2 + 1) * 1024]
                        ).then_inc(SM["qP2a" if kk % 2 == 0 else "qP2b"], 16)
            kk = 47
            sy.wait_ge(SM["qV2"], 48)
            sy.dma_start(
                out=bass.AP(p2_d, ((kk // 12) * IG + 2 * (kk % 12)) * TILE,
                            [[TCH, 128], [TILE, 2], [1, TCH]]),
                in_=p2b[:, (kk % 2) * 1024:(kk % 2 + 1) * 1024]
                ).then_inc(SM["qP2a" if kk % 2 == 0 else "qP2b"], 16)

        @blk.scalar
        def _(s):
            s = Chain(s, "sself")
            for k in range(48):
                r = k % 2
                s.wait_ge(SM["qLu0" if r == 0 else "qLu1"], 16 * (k // 2 + 1))
                s.wait_ge(SM["qBCu"], 32)
                s.wait_ge(SM["qCNv"], 2)
                if k >= 1:
                    s.wait_ge(SM["qV2"], k)
                s.activation(nupb[:, :], ub[:, r * 1024:(r + 1) * 1024], AF.Identity,
                             scale=bc[:, BC_CUP:BC_CUP + 1], bias=cstP[:, 0:1])
                s.activation(nupb[:, :], nupb[:, :], AF.Identity,
                             scale=cstP[:, 2:3], bias=cstP[:, 1:2]).then_inc("qNUP", 1)

        @blk.vector
        def _(v):
            v = Chain(v, "vself")
            v.memset(cstP[:, 0:1], MAGIC)
            v.memset(cstP[:, 1:2], -MAGIC)
            v.memset(cstP[:, 2:3], 1.0)
            v.memset(mxt[:, MX_P2:MX_P2 + 1], 0.0).then_inc("qCNv", 1)
            for k in range(48):
                r = k % 2
                ra, rb = r * 1024, (r + 1) * 1024
                v.wait_ge(SM["qLp0" if r == 0 else "qLp1"], 16 * (k // 2 + 1))
                v.wait_ge(SM["qBCu"], 32)
                v.tensor_scalar(pbf[:, ra:rb], pbf[:, ra:rb], bc[:, BC_C2:BC_C2 + 1],
                                MAGIC, ALU.mult, ALU.add)
                v.tensor_scalar(pbf[:, ra:rb], pbf[:, ra:rb], MAGIC, None, ALU.subtract)
                v.wait_ge(SM["qNUP"], k + 1)
                if k >= 2:
                    v.wait_ge(SM["qP2a" if k % 2 == 0 else "qP2b"], 16 * (k // 2))
                v.tensor_mul(p2b[:, ra:rb], pbf[:, ra:rb], nupb[:, :])
                v.reduce_max(tmx3[:, :], p2b[:, ra:rb], axis=AX, apply_absolute_value=True)
                v.tensor_max(mxt[:, MX_P2:MX_P2 + 1], mxt[:, MX_P2:MX_P2 + 1],
                             tmx3[:, :]).then_inc("qV2", 1)
            v.engine_nop().then_inc("qMX2", 1)
            v.wait_ge(SM["qB3"], 32)
            v.reduce_max(sc1[0:1, 10:11], mrow[0:1, 0:128], axis=AX)
            v.tensor_mul(sc1[0:1, 10:11], sc1[0:1, 10:11], sc1[0:1, 13:14])
            v.tensor_scalar(sc1[0:1, 9:10], sc1[0:1, 10:11], 1.0 / QMAX, 1e-12,
                            ALU.mult, ALU.max)
            v.reciprocal(sc1[0:1, 12:13], sc1[0:1, 9:10])
            v.tensor_mul(sc1[0:1, 12:13], sc1[0:1, 12:13], sc1[0:1, 13:14]).then_inc("qMX2", 1)

        @blk.gpsimd
        def _(g):
            g.wait_ge(SM["qMX2"], 1)
            g.dma_start(out=bass.AP(cc3_i, 0, [[1, 1], [1, 128]]),
                        in_=mxt[:, MX_P2:MX_P2 + 1]).then_inc(SM["qB3"], 16)
            g.wait_ge(SM["qB3"], 16)
            g.collective_compute("AllReduce", ALU.max, replica_groups=[list(range(8))],
                                 ins=[cc3_i.ap().opt()], outs=[cc3_o.ap().opt()]).then_inc(SM["qC3"], 1)
            g.wait_ge(SM["qC3"], 1)
            g.dma_start(out=mrow[0:1, 0:128], in_=cc3_o.ap()).then_inc(SM["qB3"], 16)
            g.wait_ge(SM["qMX2"], 2)
            g.dma_start(out=bass.AP(bcast_d, 5, [[1, 1], [1, 1]]),
                        in_=sc1[0:1, 9:10]).then_inc(SM["qB3"], 16)
            g.dma_start(out=bass.AP(bcast_d, 6, [[1, 1], [1, 1]]),
                        in_=sc1[0:1, 12:13]).then_inc(SM["qB3"], 16)
            g.wait_ge(SM["qB3"], 64)
            g.dma_start(out=bc[:, BC_S5:BC_S5 + 1],
                        in_=bass.AP(bcast_d, 5, [[0, 128], [1, 1]])).then_inc(SM["qBC3"], 16)
            g.dma_start(out=bc[:, BC_C3:BC_C3 + 1],
                        in_=bass.AP(bcast_d, 6, [[0, 128], [1, 1]])).then_inc(SM["qBC3"], 16)
    esC2.close()

    esC3 = ExitStack()
    dnA = esC3.enter_context(nc.sbuf_tensor("dnA", [128, 2 * IG * TCH], BF16))
    dnB = esC3.enter_context(nc.sbuf_tensor("dnB", [128, 2 * IG * TCH], BF16))
    wdh = esC3.enter_context(nc.sbuf_tensor("wdh", [128, 2 * 3072], BF16))
    wdl = esC3.enter_context(nc.sbuf_tensor("wdl", [128, 2 * 3072], BF16))
    p2r = esC3.enter_context(nc.sbuf_tensor("p2r", [128, 2 * 1024], F32))
    nob = esC3.enter_context(nc.sbuf_tensor("nob", [128, 1024], F32))
    xrb = esC3.enter_context(nc.sbuf_tensor("xrb", [128, G8 * TCH], F32))
    h2b = esC3.enter_context(nc.sbuf_tensor("h2b", [128, G8 * TCH], F32))
    sqt2 = esC3.enter_context(nc.sbuf_tensor("sqt2", [128, TCH], F32))
    cstQ = esC3.enter_context(nc.sbuf_tensor("cstQ", [128, 4], F32))

    with nc.Block() as blk:
        @blk.sync
        def _(sy):
            sy.wait_ge(SM["qP2a"], 16 * 24)
            sy.wait_ge(SM["qP2b"], 16 * 24)
            sy.wait_ge(SM["qWQS"], 16 * 48)
            for tc in range(NTC):
                if tc >= 1:
                    sy.wait_ge(SM["qOE2"], tc * G8)
                sy.dma_start(out=xrb[:, 0:4096],
                             in_=bass.AP(xln_d, tc * TCH,
                                         [[TH, 128], [128 * TH, G8], [1, TCH]])
                             ).then_inc(SM["qLx20"], 16)
                for i2 in range(12):
                    k = tc * 12 + i2
                    r = k % 2
                    if k >= 2:
                        sy.wait_ge(SM["qNO"], k - 1)
                    sy.dma_start(out=p2r[:, r * 1024:(r + 1) * 1024],
                                 in_=bass.AP(p2_d, (tc * IG + 2 * i2) * TILE,
                                             [[TCH, 128], [TILE, 2], [1, TCH]])
                                 ).then_inc(SM["qLq0" if r == 0 else "qLq1"], 16)
                for og in range(G8):
                    m = tc * G8 + og
                    r = m % 2
                    if m >= 2:
                        sy.wait_ge(SM["qMM2"], m - 1)
                    sy.dma_start(out=wdh[:, r * 3072:(r + 1) * 3072],
                                 in_=bass.AP(wqd_hi, og * 16384,
                                             [[128, 128], [G8 * 16384, IG], [1, 128]])
                                 ).then_inc(SM["qLW0" if r == 0 else "qLW1"], 16)
                    sy.dma_start(out=wdl[:, r * 3072:(r + 1) * 3072],
                                 in_=bass.AP(wqd_lo, og * 16384,
                                             [[128, 128], [G8 * 16384, IG], [1, 128]])
                                 ).then_inc(SM["qLW0" if r == 0 else "qLW1"], 16)
                sy.wait_ge(SM["qLAv"], 2 * NTC + 2 * tc + 2)
                for g in range(G8):
                    sy.dma_start(
                        out=bass.AP(out_fm.tensor, g * 128 * TH + tc * TCH,
                                    [[TH, 128], [1, TCH]]),
                        in_=h2b[:, g * TCH:(g + 1) * TCH]).then_inc(SM["qST"], 16)
            sy.wait_ge(SM["qST"], ST0 + 16 * 32)

        @blk.vector
        def _(v):
            v = Chain(v, "vself")
            v.memset(cstQ[:, 0:1], MAGIC)
            v.memset(cstQ[:, 1:2], -MAGIC).then_inc("qCNv", 1)
            for tc in range(NTC):
                rx = (tc % 2) * 4096
                for i2 in range(12):
                    k = tc * 12 + i2
                    r = k % 2
                    ra, rb = r * 1024, (r + 1) * 1024
                    v.wait_ge(SM["qLq0" if r == 0 else "qLq1"], 16 * (k // 2 + 1))
                    v.wait_ge(SM["qBC3"], 32)
                    if tc >= 2 and i2 == 0:
                        v.wait_ge(SM["qMM2"], (tc - 1) * G8)
                    v.tensor_scalar(nob[:, :], p2r[:, ra:rb], bc[:, BC_C3:BC_C3 + 1],
                                    MAGIC, ALU.mult, ALU.add)
                    v.tensor_scalar(nob[:, :], nob[:, :], MAGIC, None, ALU.subtract)
                    v.tensor_scalar(p2r[:, ra:rb], nob[:, :], 1.0 / 256.0, MAGIC,
                                    ALU.mult, ALU.add)
                    dst = (tc % 2) * IG * TCH + 2 * i2 * TCH
                    v.tensor_scalar(dnA[:, dst:dst + 1024], p2r[:, ra:rb], MAGIC, 256.0,
                                    ALU.subtract, ALU.mult)
                    v.scalar_tensor_tensor(dnB[:, dst:dst + 1024], dnA[:, dst:dst + 1024],
                                           -1.0, nob[:, :], ALU.mult, ALU.add).then_inc("qNO", 1)
                for og in range(G8):
                    m = tc * G8 + og
                    v.wait_ge(SM["qMM2"], m + 1)
                    v.wait_ge(SM["qBC3"], 32)
                    if tc >= 1 and og == 0:
                        v.wait_ge(SM["qST"], ST0 + 16 * G8 * tc)
                    v.wait_ge(SM["qLx20"], 16 * (tc + 1))
                    v.scalar_tensor_tensor(h2b[:, og * TCH:(og + 1) * TCH], pb[m % 2][:, :],
                                           bc[:, BC_S5:BC_S5 + 1], xrb[:, og * TCH:(og + 1) * TCH],
                                           ALU.mult, ALU.add).then_inc("qOE2", 1)
                v.wait_ge(SM["qSM"], G8 * (NTC + tc + 1))
                v.tensor_scalar_mul(lnt[0:1, 0:TCH], pb[4][0:1, :], 1.0 / C)
                v.tensor_scalar_mul(lnt[0:1, TCH:2 * TCH], pb[5][0:1, :], 1.0 / C)
                v.tensor_mul(lnt[0:1, 2 * TCH:3 * TCH], lnt[0:1, 0:TCH], lnt[0:1, 0:TCH])
                v.tensor_sub(lnt[0:1, TCH:2 * TCH], lnt[0:1, TCH:2 * TCH],
                             lnt[0:1, 2 * TCH:3 * TCH]).then_inc("qLAv", 1)
                v.wait_ge(SM["qLAs"], NTC + tc + 1)
                v.tensor_mul(lnt[0:1, 3 * TCH:4 * TCH], lnt[0:1, 0:TCH],
                             lnt[0:1, 2 * TCH:3 * TCH]).then_inc("qLB", 1)
                v.wait_ge(SM["qLS"], NTC + tc + 1)
                for g in range(G8):
                    xs = h2b[:, g * TCH:(g + 1) * TCH]
                    v.tensor_mul(xs, xs, pb[6][:, :])
                    v.tensor_sub(xs, xs, pb[7][:, :])
                v.engine_nop().then_inc("qLAv", 1)

        @blk.scalar
        def _(s):
            s = Chain(s, "sself")
            for tc in range(NTC):
                for g in range(G8):
                    idx = NTC * G8 + tc * G8 + g
                    s.wait_ge(SM["qOE2"], tc * G8 + g + 1)
                    if idx >= 1:
                        s.wait_ge(SM["qSM"], idx)
                    s.activation(sqt2[:, :], h2b[:, g * TCH:(g + 1) * TCH],
                                 AF.Square).then_inc("qSQ", 1)
                s.wait_ge(SM["qLAv"], 2 * NTC + 2 * tc + 1)
                s.activation(lnt[0:1, 2 * TCH:3 * TCH], lnt[0:1, TCH:2 * TCH], AF.Ln,
                             bias=cst[0:1, CS_EPS:CS_EPS + 1])
                s.activation(lnt[0:1, 2 * TCH:3 * TCH], lnt[0:1, 2 * TCH:3 * TCH], AF.Exp,
                             scale=cst[0:1, CS_NHALF:CS_NHALF + 1]).then_inc("qLAs", 1)

        @blk.tensor
        def _(t):
            for tc in range(NTC):
                for og in range(G8):
                    m = tc * G8 + og
                    r = m % 2
                    t.wait_ge(SM["qNO"], 12 * (tc + 1))
                    t.wait_ge(SM["qLW0" if r == 0 else "qLW1"], 32 * (m // 2 + 1))
                    if m >= 2:
                        t.wait_ge(SM["qOE2"], m - 1)
                    bank = pb[m % 2]
                    for kg in range(IG):
                        lh = wdh[:, r * 3072 + kg * 128:r * 3072 + (kg + 1) * 128]
                        ll = wdl[:, r * 3072 + kg * 128:r * 3072 + (kg + 1) * 128]
                        dsrc = (tc % 2) * IG * TCH + kg * TCH
                        rh = dnA[:, dsrc:dsrc + TCH]
                        rl = dnB[:, dsrc:dsrc + TCH]
                        t.matmul(bank[:, :], lh, rh, start=(kg == 0), stop=False)
                        t.matmul(bank[:, :], lh, rl, start=False, stop=False)
                        mm = t.matmul(bank[:, :], ll, rh, start=False, stop=(kg == IG - 1))
                        if kg == IG - 1:
                            mm.then_inc(SM["qMM2"], 1)
                for g in range(G8):
                    t.wait_ge(SM["qOE2"], tc * G8 + g + 1)
                    xs = h2b[:, g * TCH:(g + 1) * TCH]
                    t.matmul(pb[4][0:1, :], ones_col[:, :], xs,
                             start=(g == 0), stop=(g == G8 - 1))
                    t.wait_ge(SM["qSQ"], NTC * G8 + tc * G8 + g + 1)
                    t.matmul(pb[5][0:1, :], ones_col[:, :], sqt2[:, :],
                             start=(g == 0), stop=(g == G8 - 1)).then_inc(SM["qSM"], 1)
                t.wait_ge(SM["qLB"], NTC + tc + 1)
                t.matmul(pb[6][:, :], ones_row[:, :], lnt[0:1, 2 * TCH:3 * TCH],
                         start=True, stop=True)
                t.matmul(pb[7][:, :], ones_row[:, :], lnt[0:1, 3 * TCH:4 * TCH],
                         start=True, stop=True).then_inc(SM["qLS"], 1)

        @blk.gpsimd
        def _(g):
            g.wait_ge(SM["qST"], ST0 + 16 * 32)
    esC3.close()

    esB.close()
    es.close()
    return nc


def host_prep(L0_attn_x, L0_attn_W_rvk, L0_attn_W_out, L0_attn_time_decay,
              L0_attn_time_first, L0_ffn_w_gate, L0_ffn_w_up, L0_ffn_w_down):
    x = np.ascontiguousarray(np.asarray(L0_attn_x), dtype=np.float32)
    W_rvk = np.asarray(L0_attn_W_rvk)
    W_out = np.asarray(L0_attn_W_out)
    td = np.asarray(L0_attn_time_decay)
    tf = np.asarray(L0_attn_time_first)
    wg = np.ascontiguousarray(np.asarray(L0_ffn_w_gate), dtype=np.float32)
    wu = np.ascontiguousarray(np.asarray(L0_ffn_w_up), dtype=np.float32)
    wd = np.ascontiguousarray(np.asarray(L0_ffn_w_down), dtype=np.float32)
    woutT = np.ascontiguousarray(W_out.T, dtype=np.float32)
    ident = np.eye(128, dtype=np.float32)
    wupT = np.ascontiguousarray(wu.T)
    wgateT = np.ascontiguousarray(wg.T)
    wdownT = np.ascontiguousarray(wd.T)
    in_maps = []
    for c in range(8):
        b, h = c // 2, c % 2
        sel = np.r_[h * CH:(h + 1) * CH, C + h * CH:C + (h + 1) * CH,
                    2 * C + h * CH:2 * C + (h + 1) * CH]
        wrvkT = np.ascontiguousarray(W_rvk[sel, :].T, dtype=np.float32)
        xt = np.ascontiguousarray(x[b].T)
        xth = np.ascontiguousarray(xt[:, h * TH:(h + 1) * TH])
        tdv = np.ascontiguousarray(td[h * CH:(h + 1) * CH].reshape(CG, 128).T,
                                   dtype=np.float32)
        tfv = np.ascontiguousarray(tf[h * CH:(h + 1) * CH].reshape(CG, 128).T,
                                   dtype=np.float32)
        agidx = np.zeros((1, G8 * NTC), dtype=np.int32)
        for tc in range(NTC):
            agidx[0, tc] = (2 * b * CG * NCH + (4 * h + tc)) * 128 * 512
        in_maps.append({
            "xT": xt, "xTh": xth, "wrvkT": wrvkT, "woutT_d": woutT,
            "tdv": tdv, "tfv": tfv, "ident_in": ident, "agidx_in": agidx,
            "wup": wu, "wupT": wupT, "wgate": wg, "wgateT": wgateT,
            "wdown": wd, "wdownT": wdownT,
        })
    return in_maps


def kernel(L0_attn_x, L0_attn_W_rvk, L0_attn_W_out, L0_attn_time_decay,
           L0_attn_time_first, L0_ffn_w_gate, L0_ffn_w_up, L0_ffn_w_down):
    in_maps = host_prep(L0_attn_x, L0_attn_W_rvk, L0_attn_W_out, L0_attn_time_decay,
                        L0_attn_time_first, L0_ffn_w_gate, L0_ffn_w_up, L0_ffn_w_down)
    nc = build_nc(debug=0)
    res = run_bass_kernel_spmd(nc, in_maps, core_ids=list(range(8)))
    out = np.empty((B, T, C), dtype=np.float32)
    for c in range(8):
        b, h = c // 2, c % 2
        out[b, h * TH:(h + 1) * TH, :] = res.results[c]["out_fm"].T
    return out


# version 03
# speedup vs baseline: 1.0219x; 1.0219x over previous
import numpy as np
from contextlib import ExitStack

import concourse.bass as bass
import concourse.mybir as mybir
from concourse.bass_utils import run_bass_kernel_spmd

F32 = mybir.dt.float32
BF16 = mybir.dt.bfloat16
I32 = mybir.dt.int32
AF = mybir.ActivationFunctionType
ALU = mybir.AluOpType
AX = mybir.AxisListType.X
MAGIC = 12582912.0

B, T, C = 4, 4096, 1024
TH = T // 2
CH = C // 2
CG = CH // 128
G8 = C // 128
TCH = 512
NCH = T // TCH
NTC = TH // TCH
I3 = 3072
IG = I3 // 128
QMAX = 32767.0

TILE = 128 * TCH


class _Collect:
    def __init__(self, store):
        self.store = store

    def __getattr__(self, eng):
        def deco(f):
            self.store.setdefault(eng, []).append(f)
        return deco


def build_nc(debug=0):
    nc = bass.Bass("TRN2", target_bir_lowering=False, debug=False)

    xT = nc.dram_tensor("xT", [C, T], F32, kind="ExternalInput").ap()
    xTh = nc.dram_tensor("xTh", [C, TH], F32, kind="ExternalInput").ap()
    wrvkT = nc.dram_tensor("wrvkT", [C, 3 * CH], F32, kind="ExternalInput").ap()
    woutT_d = nc.dram_tensor("woutT_d", [C, C], F32, kind="ExternalInput").ap()
    tdv = nc.dram_tensor("tdv", [128, CG], F32, kind="ExternalInput").ap()
    tfv = nc.dram_tensor("tfv", [128, CG], F32, kind="ExternalInput").ap()
    ident_in = nc.dram_tensor("ident_in", [128, 128], F32, kind="ExternalInput").ap()
    agidx_in = nc.dram_tensor("agidx_in", [1, G8 * NTC], I32, kind="ExternalInput").ap()
    wup = nc.dram_tensor("wup", [I3, C], F32, kind="ExternalInput").ap()
    wupT = nc.dram_tensor("wupT", [C, I3], F32, kind="ExternalInput").ap()
    wgate = nc.dram_tensor("wgate", [I3, C], F32, kind="ExternalInput").ap()
    wgateT = nc.dram_tensor("wgateT", [C, I3], F32, kind="ExternalInput").ap()
    wdown = nc.dram_tensor("wdown", [C, I3], F32, kind="ExternalInput").ap()
    wdownT = nc.dram_tensor("wdownT", [I3, C], F32, kind="ExternalInput").ap()

    out_fm = nc.dram_tensor("out_fm", [C, TH], F32, kind="ExternalOutput").ap()
    dbg_kind = "ExternalOutput" if debug else "Internal"
    dwkv = nc.dram_tensor("dwkv", [CH, T], F32, kind=dbg_kind).ap() if debug else None
    dxln = nc.dram_tensor("dxln", [C, TH], F32, kind=dbg_kind).ap() if debug else None

    ag_in = nc.dram_tensor("ag_in", [CG, NCH, 128, TCH], F32)
    ag_out = nc.dram_tensor("ag_out", [8, CG, NCH, 128, TCH], F32)
    cc0_i = nc.dram_tensor("cc0_i", [1, 128], F32)
    cc0_o = nc.dram_tensor("cc0_o", [1, 128], F32)
    ccg_i = nc.dram_tensor("ccg_i", [1, 128], F32)
    ccg_o = nc.dram_tensor("ccg_o", [1, 128], F32)
    ccu_i = nc.dram_tensor("ccu_i", [2, 128], F32)
    ccu_o = nc.dram_tensor("ccu_o", [2, 128], F32)
    cc3_i = nc.dram_tensor("cc3_i", [1, 128], F32)
    cc3_o = nc.dram_tensor("cc3_o", [1, 128], F32)
    up_d = nc.dram_tensor("up_d", [NTC, IG, 128, TCH], F32)
    gate_d = nc.dram_tensor("gate_d", [NTC, IG, 128, TCH], F32)
    p_d = nc.dram_tensor("p_d", [NTC, IG, 128, TCH], F32)
    p2_d = nc.dram_tensor("p2_d", [NTC, IG, 128, TCH], F32)
    xln_d = nc.dram_tensor("xln_d", [G8, 128, TH], F32)
    mx_d = nc.dram_tensor("mx_d", [1, 128], F32)
    bcast_d = nc.dram_tensor("bcast_d", [1, 16], F32)
    scl_up = nc.dram_tensor("scl_up", [2, 32, 32, I3], F32)
    scl_gate = nc.dram_tensor("scl_gate", [2, 32, 32, I3], F32)
    scl_down = nc.dram_tensor("scl_down", [2, 96, 32, C], F32)
    wqd_hi = nc.dram_tensor("wqd_hi", [G8, IG, 128, 128], BF16)
    wqd_lo = nc.dram_tensor("wqd_lo", [G8, IG, 128, 128], BF16)

    es = ExitStack()
    SECT = {}

    names = ["qP0", "qXL", "qXH", "qXS", "qWL", "qWH", "qWS", "qTE", "qSE", "qVC", "qRD", "qSL", "qSV", "qAG", "qOR", "qOH", "qOL", "qOM", "qOE", "qSQ", "qLS", "qLB", "qWO1", "qWO2", "qXR", "qTT", "qTS", "qTD", "qNX", "qNO", "qMM2", "qEV2", "qST", "qRB", "qSM", "vself", "sself", "qLAv", "qLAs", "qCNv", "qCNs", "qOR1", "qOH1", "qXLS", "qB0", "qBg", "qBu", "qB3", "qRB0", "qRB1", "qWT0", "qWT1", "qDQ", "qMMg", "qMMu", "qEVg", "qEVu", "qSP20", "qSP21", "qLg0", "qLg1", "qLu0", "qLu1", "qLp0", "qLp1", "qLq0", "qLq1", "qLx20", "qLx21", "qLW0", "qLW1", "qWQS", "qCHg", "qMXg", "qMXu", "qMX2", "qV2", "qDN", "qLxl0", "qLxl1", "qBC0", "qBCg", "qCHa", "qSXg", "qMXt", "qMXt2", "qNUP", "qP2a", "qP2b", "qOE2", "qBCu", "qBC3", "qS0", "qS1", "qS2", "qS3", "qC0", "qCg", "qCu", "qC3", "qCA"]
    SM = {n: es.enter_context(nc.semaphore(n)) for n in names}

    TOT = {n: 0 for n in names}
    CHST = {}

    class ChProxy:
        def __init__(self, ch, ins):
            self.ch, self.ins = ch, ins

        def then_inc(self, semname, k=1):
            TOT[semname] += k
            self.ins.then_inc(SM[semname], k)
            self.ch.pend_inc = (SM[semname], TOT[semname])
            return self

    class Chain:
        def __init__(self, eng, key):
            self.e = eng
            self.key = key
            st = CHST.setdefault(key, {"last": None, "pend": None, "pend_inc": None})
            self.st = st

        @property
        def pend_inc(self):
            return self.st["pend_inc"]

        @pend_inc.setter
        def pend_inc(self, v):
            self.st["pend_inc"] = v

        def _flush(self):
            st = self.st
            if st["pend"] is not None:
                if st["pend_inc"] is None:
                    nm = self.key
                    TOT[nm] += 1
                    st["pend"].then_inc(SM[nm], 1)
                    st["last"] = (SM[nm], TOT[nm])
                else:
                    st["last"] = st["pend_inc"]
                st["pend"] = None
                st["pend_inc"] = None

        def wait_ge(self, sem, val):
            self._flush()
            if val > 0:
                self.e.wait_ge(sem, val)

        def __getattr__(self, m):
            f = getattr(self.e, m)
            if not callable(f):
                return f
            ch = self

            def wrap(*a, **kw):
                ch._flush()
                if ch.st["last"] is not None:
                    ch.e.wait_ge(*ch.st["last"])
                ins = f(*a, **kw)
                ch.st["pend"] = ins
                return ChProxy(ch, ins)
            return wrap

    ident = es.enter_context(nc.sbuf_tensor("ident", [128, 128], F32))
    ones_col = es.enter_context(nc.sbuf_tensor("ones_col", [128, 1], F32))
    ones_row = es.enter_context(nc.sbuf_tensor("ones_row", [1, 128], F32))
    euw = es.enter_context(nc.sbuf_tensor("euw", [128, 2 * CG], F32))
    bc = es.enter_context(nc.sbuf_tensor("bc", [128, 16], F32))
    mxt = es.enter_context(nc.sbuf_tensor("mxt", [128, 8], F32))
    sc1 = es.enter_context(nc.sbuf_tensor("sc1", [1, 16], F32))
    cst = es.enter_context(nc.sbuf_tensor("cst", [1, 8], F32))
    agix = es.enter_context(nc.sbuf_tensor("agix", [1, G8 * NTC], I32))
    sAll = es.enter_context(nc.sbuf_tensor("sAll", [128, 2 * (768 + 768 + 768)], F32))
    stg96 = es.enter_context(nc.sbuf_tensor("stg96", [96, 128], F32))
    lnt = es.enter_context(nc.sbuf_tensor("lnt", [1, 4 * TCH], F32))
    mrow = es.enter_context(nc.sbuf_tensor("mrow", [1, 256], F32))
    pb = [es.enter_context(nc.psum_tensor(f"pb{i}", [128, TCH], F32)) for i in range(8)]

    BC_R1, BC_CG, BC_S3, BC_CUP, BC_C2, BC_C3, BC_S5 = range(7)
    MX_X, MX_UP, MX_P, MX_P2 = range(4)
    SC_S1, SC_S3, SC_K4, SC_S2, SC_S4, SC_K5, SC_S5 = range(7)
    CS_EPS, CS_NHALF, CS_ONE = 0, 1, 2

    SOFF = {"up": 0, "gate": 1536, "down": 3072}
    NBLK = {"up": 32, "gate": 32, "down": 96}
    NOST = {"up": 24, "gate": 24, "down": 8}
    NATW = {"up": wup, "gate": wgate, "down": wdown}
    SCLD = {"up": scl_up, "gate": scl_gate, "down": scl_down}

    if True:
        blk = _Collect(SECT)
        @blk.sync
        def _(sy):
            sy.dma_start(out=ident[:, :], in_=ident_in).then_inc(SM["qP0"], 16)
            sy.dma_start(out=euw[:, 0:CG], in_=tdv).then_inc(SM["qP0"], 16)
            sy.dma_start(out=euw[:, CG:2 * CG], in_=tfv).then_inc(SM["qP0"], 16)
            sy.dma_start(out=agix[:, :], in_=agidx_in).then_inc(SM["qP0"], 16)

        @blk.vector
        def _(v):
            v = Chain(v, "vself")
            v.memset(ones_col[:, :], 1.0)
            v.memset(ones_row[:, :], 1.0)
            v.memset(mxt[:, :], 0.0)
            v.memset(cst[:, CS_EPS:CS_EPS + 1], 1e-5)
            v.memset(cst[:, CS_NHALF:CS_NHALF + 1], -0.5)
            v.memset(cst[:, CS_ONE:CS_ONE + 1], 1.0).then_inc("qCNv", 1)

        @blk.scalar
        def _(s):
            s = Chain(s, "sself")
            s.wait_ge(SM["qP0"], 64)
            s.activation(euw[:, 0:CG], euw[:, 0:CG], AF.Exp)
            s.activation(euw[:, 0:CG], euw[:, 0:CG], AF.Exp, scale=-1.0)
            s.activation(euw[:, CG:2 * CG], euw[:, CG:2 * CG], AF.Exp).then_inc("qCNs", 1)

    esA = ExitStack()
    wrvk_f = esA.enter_context(nc.sbuf_tensor("wrvk_f", [128, 3 * CH], F32))
    wrvk_hi = esA.enter_context(nc.sbuf_tensor("wrvk_hi", [128, G8 * 3 * CH], BF16))
    wrvk_lo = esA.enter_context(nc.sbuf_tensor("wrvk_lo", [128, G8 * 3 * CH], BF16))
    xf = esA.enter_context(nc.sbuf_tensor("xf", [128, G8 * TCH], F32))
    xhi = esA.enter_context(nc.sbuf_tensor("xhi", [128, 2 * G8 * TCH], BF16))
    xlo = esA.enter_context(nc.sbuf_tensor("xlo", [128, 2 * G8 * TCH], BF16))
    vbuf = esA.enter_context(nc.sbuf_tensor("vbuf", [128, CG * TCH], F32))
    ekb = esA.enter_context(nc.sbuf_tensor("ekb", [128, CG * (TCH + 1)], F32))
    ekvb = esA.enter_context(nc.sbuf_tensor("ekvb", [128, CG * (TCH + 1)], F32))
    ab = esA.enter_context(nc.sbuf_tensor("ab", [128, CG * (TCH + 1)], F32))
    bbf = esA.enter_context(nc.sbuf_tensor("bbf", [128, CG * (TCH + 1)], F32))
    numb = esA.enter_context(nc.sbuf_tensor("numb", [128, CG * TCH], F32))
    denb = esA.enter_context(nc.sbuf_tensor("denb", [128, CG * TCH], F32))
    erb = esA.enter_context(nc.sbuf_tensor("erb", [128, CG * TCH], F32))
    wnat = esA.enter_context(nc.sbuf_tensor("wnat", [128, I3], F32))
    sblk = esA.enter_context(nc.sbuf_tensor("sblk", [128, 96], F32))

    def s3d(t, off, strideg, ng, n):
        return bass.AP(t.ap().tensor, off, [[t.shape[1], 128], [strideg, ng], [1, n]])

    strip_sched = [(w, o) for w in ("gate", "up", "down") for o in range(NOST[w])]

    if True:
        blk = _Collect(SECT)
        @blk.sync
        def _(sy):
            for cg in range(G8):
                if cg >= 1:
                    sy.wait_ge(SM["qWS"], cg)
                sy.dma_start(out=wrvk_f[:, :],
                             in_=wrvkT[cg * 128:(cg + 1) * 128, :]).then_inc(SM["qWL"], 16)
            for j in range(NCH):
                if j >= 1:
                    sy.wait_ge(SM["qXS"], j)
                sy.dma_start(
                    out=xf[:, :],
                    in_=bass.AP(xT.tensor, j * TCH,
                                [[T, 128], [128 * T, G8], [1, TCH]])).then_inc(SM["qXL"], 16)
                for k in range(7 * j, min(7 * (j + 1), len(strip_sched))):
                    w, o = strip_sched[k]
                    iw = C if w in ("up", "gate") else I3
                    if k >= 1:
                        sy.wait_ge(SM["qSV"], k)
                    sy.dma_start(out=wnat[:, 0:iw],
                                 in_=NATW[w][o * 128:(o + 1) * 128, :]).then_inc(SM["qSL"], 16)

        @blk.gpsimd
        def _(g):
            for cg in range(G8):
                g.wait_ge(SM["qWL"], 16 * (cg + 1))
                g.dma_start(out=wrvk_hi[:, cg * 3 * CH:(cg + 1) * 3 * CH],
                            in_=wrvk_f[:, :]).then_inc(SM["qWH"], 16)
            for j in range(NCH):
                g.wait_ge(SM["qXL"], 16 * (j + 1))
                if j >= 2:
                    g.wait_ge(SM["qTE"], 12 * (j - 1))
                rx = (j % 2) * 4096
                g.dma_start(out=xhi[:, rx:rx + 4096], in_=xf[:, :]).then_inc(SM["qXH"], 16)
                g.wait_ge(SM["qVC"], j + 1)
                g.dma_start(
                    out=bass.AP(ag_in, j * 128 * TCH,
                                [[TCH, 128], [NCH * 128 * TCH, CG], [1, TCH]]),
                    in_=numb[:, :]).then_inc(SM["qRD"], 16)
                if debug:
                    g.dma_start(
                        out=bass.AP(dwkv.tensor, j * TCH,
                                    [[T, 128], [128 * T, CG], [1, TCH]]),
                        in_=numb[:, :]).then_inc(SM["qRD"], 16)

        @blk.vector
        def _(v):
            v = Chain(v, "vself")
            for cg in range(G8):
                v.wait_ge(SM["qWH"], 16 * (cg + 1))
                v.tensor_sub(wrvk_lo[:, cg * 3 * CH:(cg + 1) * 3 * CH], wrvk_f[:, :],
                             wrvk_hi[:, cg * 3 * CH:(cg + 1) * 3 * CH]).then_inc("qWS", 1)
            v.wait_ge(SM["qCNs"], 1)
            for g in range(CG):
                v.memset(ekvb[:, g * (TCH + 1):g * (TCH + 1) + 1], 0.0)
                v.memset(ekb[:, g * (TCH + 1):g * (TCH + 1) + 1], 0.0)
            for j in range(NCH):
                v.wait_ge(SM["qXH"], 16 * (j + 1))
                rx = (j % 2) * 4096
                v.tensor_sub(xlo[:, rx:rx + 4096], xf[:, :], xhi[:, rx:rx + 4096]).then_inc("qXS", 1)
                v.wait_ge(SM["qSE"], 12 * (j + 1))
                if j >= 1:
                    v.wait_ge(SM["qRD"], 16 * j * (2 if debug else 1))
                    v.tensor_copy(s3d(ekvb, 0, TCH + 1, CG, 1), s3d(ab, TCH, TCH + 1, CG, 1))
                    v.tensor_copy(s3d(ekb, 0, TCH + 1, CG, 1), s3d(bbf, TCH, TCH + 1, CG, 1))
                v.tensor_mul(s3d(ekvb, 1, TCH + 1, CG, TCH), s3d(ekb, 1, TCH + 1, CG, TCH),
                             s3d(vbuf, 0, TCH, CG, TCH))
                for g in range(CG):
                    o = g * (TCH + 1)
                    v.tensor_tensor_scan(ab[:, o:o + TCH + 1],
                                         euw[:, g:g + 1].broadcast_to([128, TCH + 1]),
                                         ekvb[:, o:o + TCH + 1], 0.0, ALU.mult, ALU.add)
                    v.tensor_tensor_scan(bbf[:, o:o + TCH + 1],
                                         euw[:, g:g + 1].broadcast_to([128, TCH + 1]),
                                         ekb[:, o:o + TCH + 1], 0.0, ALU.mult, ALU.add)
                for g in range(CG):
                    o = g * (TCH + 1)
                    q = g * TCH
                    v.scalar_tensor_tensor(numb[:, q:q + TCH], ekvb[:, o + 1:o + TCH + 1],
                                           euw[:, CG + g:CG + g + 1], ab[:, o:o + TCH],
                                           ALU.mult, ALU.add)
                    v.scalar_tensor_tensor(denb[:, q:q + TCH], ekb[:, o + 1:o + TCH + 1],
                                           euw[:, CG + g:CG + g + 1], bbf[:, o:o + TCH],
                                           ALU.mult, ALU.add)
                v.reciprocal(denb[:, :], denb[:, :])
                v.tensor_mul(numb[:, :], numb[:, :], denb[:, :])
                v.tensor_scalar_add(erb[:, :], erb[:, :], 1.0)
                v.reciprocal(erb[:, :], erb[:, :])
                v.tensor_mul(numb[:, :], numb[:, :], erb[:, :]).then_inc("qVC", 1)
                for k in range(7 * j, min(7 * (j + 1), len(strip_sched))):
                    w, o = strip_sched[k]
                    nb = NBLK[w]
                    v.wait_ge(SM["qSL"], 16 * (k + 1))
                    v.reduce_max(
                        bass.AP(sblk.ap().tensor, 0, [[96, 128], [1, nb], [0, 1]]),
                        bass.AP(wnat.ap().tensor, 0, [[I3, 128], [32, nb], [1, 32]]),
                        axis=AX, apply_absolute_value=True)
                    soff = SOFF[w]
                    v.tensor_scalar(sAll[:, soff + 768 + o * nb: soff + 768 + (o + 1) * nb],
                                    sblk[:, 0:nb], 1.0 / 7.0, 1e-12, ALU.mult,
                                    ALU.max).then_inc("qSV", 1)

        @blk.tensor
        def _(t):
            t.wait_ge(SM["qWH"], 16 * G8)
            t.wait_ge(SM["qWS"], G8)
            fills = 0
            for j in range(NCH):
                t.wait_ge(SM["qXS"], j + 1)
                for og in range(12):
                    if fills >= 4:
                        t.wait_ge(SM["qSE"], fills - 3)
                    bank = pb[fills % 4]
                    for kg in range(G8):
                        lh = wrvk_hi[:, kg * 3 * CH + og * 128: kg * 3 * CH + (og + 1) * 128]
                        ll = wrvk_lo[:, kg * 3 * CH + og * 128: kg * 3 * CH + (og + 1) * 128]
                        rx = (j % 2) * 4096
                        rh = xhi[:, rx + kg * TCH:rx + (kg + 1) * TCH]
                        rl = xlo[:, rx + kg * TCH:rx + (kg + 1) * TCH]
                        t.matmul(bank[:, :], lh, rh, start=(kg == 0), stop=False)
                        t.matmul(bank[:, :], lh, rl, start=False, stop=False)
                        mm = t.matmul(bank[:, :], ll, rh, start=False,
                                      stop=(kg == G8 - 1))
                        if kg == G8 - 1:
                            mm.then_inc(SM["qTE"], 1)
                    fills += 1

        @blk.scalar
        def _(s):
            s = Chain(s, "sself")
            fills = 0
            for j in range(NCH):
                if j >= 1:
                    s.wait_ge(SM["qVC"], j)
                for og in range(12):
                    s.wait_ge(SM["qTE"], fills + 1)
                    bank = pb[fills % 4]
                    g4 = og % 4
                    if og < 4:
                        ins = s.activation(erb[:, g4 * TCH:(g4 + 1) * TCH], bank[:, :],
                                           AF.Exp, scale=-1.0)
                    elif og < 8:
                        ins = s.copy(vbuf[:, g4 * TCH:(g4 + 1) * TCH], bank[:, :])
                    else:
                        ins = s.activation(ekb[:, g4 * (TCH + 1) + 1:(g4 + 1) * (TCH + 1)],
                                           bank[:, :], AF.Exp)
                    ins.then_inc("qSE", 1)
                    fills += 1

    esA.close()
    nAG = 16 * NCH * (2 if debug else 1)
    esB = ExitStack()
    xres = esB.enter_context(nc.sbuf_tensor("xres", [128, G8 * TH], F32))
    wo_f = esB.enter_context(nc.sbuf_tensor("wo_f", [128, C], F32))
    wo_hi = esB.enter_context(nc.sbuf_tensor("wo_hi", [128, G8 * C], BF16))
    wo_lo = esB.enter_context(nc.sbuf_tensor("wo_lo", [128, G8 * C], BF16))
    rrf = esB.enter_context(nc.sbuf_tensor("rrf", [128, 2 * G8 * TCH], F32))
    rrh = esB.enter_context(nc.sbuf_tensor("rrh", [128, 2 * G8 * TCH], BF16))
    rrl = esB.enter_context(nc.sbuf_tensor("rrl", [128, 2 * G8 * TCH], BF16))
    sqt = esB.enter_context(nc.sbuf_tensor("sqt", [128, TCH], F32))
    tmx = esB.enter_context(nc.sbuf_tensor("tmx", [128, 1], F32))
    NSCT = 2 * (NOST["up"] + NOST["gate"] + NOST["down"])

    if True:
        blk = _Collect(SECT)
        @blk.gpsimd
        def _(g):
            g.wait_ge(SM["qRD"], nAG)
            g.collective_compute(
                "AllGather", ALU.bypass, replica_groups=[list(range(8))],
                ins=[ag_in.ap().opt()], outs=[ag_out.ap().opt()]).then_inc(SM["qAG"], 1)
            nt = 0
            for w in ("gate", "up", "down"):
                nb = NBLK[w]
                ow = I3 if w in ("up", "gate") else C
                for o in range(NOST[w]):
                    for half in range(2):
                        g.wait_ge(SM["qTS"], nt + 1)
                        g.dma_start(
                            out=bass.AP(SCLD[w], half * nb * 32 * ow + o * 128,
                                        [[32 * ow, nb], [ow, 32], [1, 128]]),
                            in_=bass.AP(stg96.ap().tensor, 0,
                                        [[128, nb], [0, 32], [1, 128]])).then_inc(SM["qTD"], 16)
                        nt += 1
            g.wait_ge(SM["qAG"], 1)
            with g.register("agoff") as roff:
                for tc in range(NTC):
                    rt = (tc % 2) * 4096
                    if tc >= 2:
                        g.wait_ge(SM["qOL"], tc - 1)
                        g.wait_ge(SM["qOM"], (tc - 1) * G8)
                    qor = "qOR" if tc % 2 == 0 else "qOR1"
                    qoh = "qOH" if tc % 2 == 0 else "qOH1"
                    g.reg_load(roff, agix[0:1, tc:tc + 1])
                    src = bass.AP(ag_out, g.snap(roff),
                                  [[TCH, 128], [CG * NCH * TILE, 2], [NCH * TILE, CG], [1, TCH]])
                    g.dma_start(out=rrf[:, rt:rt + 4096], in_=src).then_inc(SM[qor], 16)
                    g.wait_ge(SM[qor], 16 * (tc // 2 + 1))
                    g.dma_start(out=rrh[:, rt:rt + 4096],
                                in_=rrf[:, rt:rt + 4096]).then_inc(SM[qoh], 16)

        @blk.sync
        def _(sy):
            sy.wait_ge(SM["qRD"], nAG)
            sy.wait_ge(SM["qTE"], 12 * NCH)
            sy.wait_ge(SM["qSE"], 12 * NCH)
            sy.wait_ge(SM["qSV"], len(strip_sched))
            for g in range(G8):
                sy.dma_start(out=xres[:, g * TH:(g + 1) * TH],
                             in_=xTh[g * 128:(g + 1) * 128, :]).then_inc(SM["qXR"], 16)
            for cg in range(G8):
                if cg >= 1:
                    sy.wait_ge(SM["qWO2"], cg)
                sy.dma_start(out=wo_f[:, :],
                             in_=woutT_d[cg * 128:(cg + 1) * 128, :]).then_inc(SM["qWO1"], 16)
            sy.wait_ge(SM["qLAv"], 2 * NTC)
            for g in range(G8):
                sy.dma_start(out=bass.AP(xln_d, g * 128 * TH, [[TH, 128], [1, TH]]),
                             in_=xres[:, g * TH:(g + 1) * TH]).then_inc(SM["qXLS"], 16)

        @blk.tensor
        def _(t):
            t.wait_ge(SM["qSV"], len(strip_sched))
            t.wait_ge(SM["qRB"], 1)
            nt = 0
            for w in ("gate", "up", "down"):
                nb = NBLK[w]
                for o in range(NOST[w]):
                    for half in range(2):
                        soff = SOFF[w] + 768 * half + o * nb
                        if nt >= 1:
                            t.wait_ge(SM["qTS"], nt)
                        t.transpose(pb[4][0:nb, 0:128], sAll[:, soff:soff + nb],
                                    ident[:, :]).then_inc(SM["qTT"], 1)
                        nt += 1
            t.wait_ge(SM["qWO2"], G8)
            t.wait_ge(SM["qSE"], 12 * NCH)
            t.wait_ge(SM["qTS"], NSCT)
            for tc in range(NTC):
                t.wait_ge(SM["qOL"], tc + 1)
                for og in range(G8):
                    if tc * G8 + og >= 2:
                        t.wait_ge(SM["qOE"], tc * G8 + og - 1)
                    bank = pb[(tc * G8 + og) % 2]
                    for kg in range(G8):
                        lh = wo_hi[:, kg * C + og * 128:kg * C + (og + 1) * 128]
                        ll = wo_lo[:, kg * C + og * 128:kg * C + (og + 1) * 128]
                        rt = (tc % 2) * 4096
                        rh = rrh[:, rt + kg * TCH:rt + (kg + 1) * TCH]
                        rl = rrl[:, rt + kg * TCH:rt + (kg + 1) * TCH]
                        t.matmul(bank[:, :], lh, rh, start=(kg == 0), stop=False)
                        t.matmul(bank[:, :], lh, rl, start=False, stop=False)
                        mm = t.matmul(bank[:, :], ll, rh, start=False,
                                      stop=(kg == G8 - 1))
                        if kg == G8 - 1:
                            mm.then_inc(SM["qOM"], 1)
                for g in range(G8):
                    t.wait_ge(SM["qOE"], tc * G8 + g + 1)
                    xs = xres[:, g * TH + tc * TCH: g * TH + (tc + 1) * TCH]
                    t.matmul(pb[4][0:1, :], ones_col[:, :], xs,
                             start=(g == 0), stop=(g == G8 - 1))
                    t.wait_ge(SM["qSQ"], tc * G8 + g + 1)
                    t.matmul(pb[5][0:1, :], ones_col[:, :], sqt[:, :],
                             start=(g == 0), stop=(g == G8 - 1)).then_inc(SM["qSM"], 1)
                t.wait_ge(SM["qLB"], tc + 1)
                t.matmul(pb[6][:, :], ones_row[:, :], lnt[0:1, 2 * TCH:3 * TCH],
                         start=True, stop=True)
                t.matmul(pb[7][:, :], ones_row[:, :], lnt[0:1, 3 * TCH:4 * TCH],
                         start=True, stop=True).then_inc(SM["qLS"], 1)

        @blk.vector
        def _(v):
            v = Chain(v, "vself")
            v.wait_ge(SM["qSV"], len(strip_sched))
            for w in ("up", "gate", "down"):
                so = SOFF[w]
                v.reciprocal(sAll[:, so:so + 768], sAll[:, so + 768:so + 1536])
            v.engine_nop().then_inc("qRB", 1)
            for cg in range(G8):
                v.wait_ge(SM["qWO1"], 16 * (cg + 1))
                v.tensor_copy(wo_hi[:, cg * C:(cg + 1) * C], wo_f[:, :])
                v.tensor_sub(wo_lo[:, cg * C:(cg + 1) * C], wo_f[:, :],
                             wo_hi[:, cg * C:(cg + 1) * C]).then_inc("qWO2", 1)
            v.wait_ge(SM["qXR"], 16 * G8)
            for tc in range(NTC):
                rt = (tc % 2) * 4096
                v.wait_ge(SM["qOH" if tc % 2 == 0 else "qOH1"], 16 * (tc // 2 + 1))
                v.tensor_sub(rrl[:, rt:rt + 4096], rrf[:, rt:rt + 4096],
                             rrh[:, rt:rt + 4096]).then_inc("qOL", 1)
                for og in range(G8):
                    v.wait_ge(SM["qOM"], tc * G8 + og + 1)
                    bank = pb[(tc * G8 + og) % 2]
                    xs = xres[:, og * TH + tc * TCH: og * TH + (tc + 1) * TCH]
                    v.tensor_add(xs, bank[:, :], xs).then_inc("qOE", 1)
                v.wait_ge(SM["qSM"], G8 * (tc + 1))
                v.tensor_scalar_mul(lnt[0:1, 0:TCH], pb[4][0:1, :], 1.0 / C)
                v.tensor_scalar_mul(lnt[0:1, TCH:2 * TCH], pb[5][0:1, :], 1.0 / C)
                v.tensor_mul(lnt[0:1, 2 * TCH:3 * TCH], lnt[0:1, 0:TCH], lnt[0:1, 0:TCH])
                v.tensor_sub(lnt[0:1, TCH:2 * TCH], lnt[0:1, TCH:2 * TCH],
                             lnt[0:1, 2 * TCH:3 * TCH]).then_inc("qLAv", 1)
                v.wait_ge(SM["qLAs"], tc + 1)
                v.tensor_mul(lnt[0:1, 3 * TCH:4 * TCH], lnt[0:1, 0:TCH],
                             lnt[0:1, 2 * TCH:3 * TCH]).then_inc("qLB", 1)
                v.wait_ge(SM["qLS"], tc + 1)
                for g in range(G8):
                    xs = xres[:, g * TH + tc * TCH: g * TH + (tc + 1) * TCH]
                    v.tensor_mul(xs, xs, pb[6][:, :])
                    v.tensor_sub(xs, xs, pb[7][:, :])
                    v.reduce_max(tmx[:, :], xs, axis=AX, apply_absolute_value=True)
                    v.tensor_max(mxt[:, MX_X:MX_X + 1], mxt[:, MX_X:MX_X + 1], tmx[:, :])
                v.engine_nop().then_inc("qLAv", 1)

        @blk.scalar
        def _(s):
            s = Chain(s, "sself")
            nt = 0
            for w in ("gate", "up", "down"):
                nb = NBLK[w]
                for o in range(NOST[w]):
                    for half in range(2):
                        s.wait_ge(SM["qTT"], nt + 1)
                        if nt >= 1:
                            s.wait_ge(SM["qTD"], 16 * nt)
                        s.copy(stg96[0:nb, :], pb[4][0:nb, 0:128]).then_inc("qTS", 1)
                        nt += 1
            for tc in range(NTC):
                for g in range(G8):
                    idx = tc * G8 + g
                    s.wait_ge(SM["qOE"], idx + 1)
                    if idx >= 1:
                        s.wait_ge(SM["qSM"], idx)
                    xs = xres[:, g * TH + tc * TCH: g * TH + (tc + 1) * TCH]
                    s.activation(sqt[:, :], xs, AF.Square).then_inc("qSQ", 1)
                s.wait_ge(SM["qLAv"], 2 * tc + 1)
                s.activation(lnt[0:1, 2 * TCH:3 * TCH], lnt[0:1, TCH:2 * TCH], AF.Ln,
                             bias=cst[0:1, CS_EPS:CS_EPS + 1])
                s.activation(lnt[0:1, 2 * TCH:3 * TCH], lnt[0:1, 2 * TCH:3 * TCH], AF.Exp,
                             scale=cst[0:1, CS_NHALF:CS_NHALF + 1]).then_inc("qLAs", 1)

    if debug:
        with nc.Block() as blk:
            @blk.sync
            def _(sy):
                sy.wait_ge(SM["qLAv"], 2 * NTC)
                for g in range(G8):
                    sy.dma_start(out=bass.AP(dxln.tensor, g * 128 * TH, [[TH, 128], [1, TH]]),
                                 in_=xres[:, g * TH:(g + 1) * TH]).then_inc(SM["qST"], 16)
                sy.wait_ge(SM["qST"], 16 * G8)

    esB.close()
    ST0 = 16 * G8 if debug else 0

    esC1 = ExitStack()
    nxa = esC1.enter_context(nc.sbuf_tensor("nxa", [128, G8 * TH], BF16))
    nxb = esC1.enter_context(nc.sbuf_tensor("nxb", [128, G8 * TH], BF16))
    xlb = esC1.enter_context(nc.sbuf_tensor("xlb", [128, 2 * TH], F32))
    nxf = esC1.enter_context(nc.sbuf_tensor("nxf", [128, TH], F32))
    wqf = esC1.enter_context(nc.sbuf_tensor("wqf", [128, 2 * 1024], F32))
    rsb = esC1.enter_context(nc.sbuf_tensor("rsb", [128, 2 * 1024], F32))
    ssb = esC1.enter_context(nc.sbuf_tensor("ssb", [128, 2 * 1024], F32))
    wqv = esC1.enter_context(nc.sbuf_tensor("wqv", [128, 2 * 1024], F32))
    wqh = esC1.enter_context(nc.sbuf_tensor("wqh", [128, 2 * 1024], BF16))
    wql = esC1.enter_context(nc.sbuf_tensor("wql", [128, 2 * 1024], BF16))
    stg = esC1.enter_context(nc.sbuf_tensor("stg", [128, 4 * TCH], F32))
    gbuf = esC1.enter_context(nc.sbuf_tensor("gbuf", [128, 2 * 1024], F32))
    ngb = esC1.enter_context(nc.sbuf_tensor("ngb", [128, 2 * 1024], F32))
    egb = esC1.enter_context(nc.sbuf_tensor("egb", [128, 1024], F32))
    ppb = esC1.enter_context(nc.sbuf_tensor("ppb", [128, 2 * 1024], F32))
    tmx2 = esC1.enter_context(nc.sbuf_tensor("tmx2", [128, 1], F32))

    cseq = [("gate", o, 0) for o in range(24)]
    for og in range(24):
        cseq.append(("up", og, 0))
        cseq.append(("down", og // 3, og % 3))
    WT_D = {"up": wupT, "gate": wgateT, "down": wdownT}
    WT_O = {"up": I3, "gate": I3, "down": C}
    MX_G = 4

    def wt_in_ap(w, og, kc):
        O = WT_O[w]
        return bass.AP(WT_D[w].tensor, kc * 8 * 128 * O + og * 128,
                       [[O, 128], [128 * O, 8], [1, 128]])

    def scl_in_ap(w, og, kc, half):
        nb = NBLK[w]
        O = WT_O[w]
        return bass.AP(SCLD[w], half * nb * 32 * O + kc * 32 * 32 * O + og * 128,
                       [[O, 128], [4 * 32 * O, 8], [1, 128]])

    def mm_piece_pos(ph, og):
        return og if ph == 0 else 24 + 2 * og

    if True:
        blk = _Collect(SECT)
        @blk.gpsimd
        def _(g):
            g.wait_ge(SM["qMXg"], 1)
            g.dma_start(out=bass.AP(cc0_i, 0, [[1, 1], [1, 128]]),
                        in_=mxt[:, MX_X:MX_X + 1]).then_inc(SM["qB0"], 16)
            g.wait_ge(SM["qB0"], 16)
            g.collective_compute("AllReduce", ALU.max, replica_groups=[list(range(8))],
                                 ins=[cc0_i.ap().opt()], outs=[cc0_o.ap().opt()]).then_inc(SM["qC0"], 1)
            g.wait_ge(SM["qC0"], 1)
            g.dma_start(out=mrow[0:1, 0:128], in_=cc0_o.ap()).then_inc(SM["qB0"], 16)
            g.wait_ge(SM["qMXg"], 2)
            g.dma_start(out=bass.AP(bcast_d, 0, [[1, 1], [1, 1]]),
                        in_=sc1[0:1, 2:3]).then_inc(SM["qB0"], 16)
            g.wait_ge(SM["qB0"], 48)
            g.dma_start(out=bc[:, BC_R1:BC_R1 + 1],
                        in_=bass.AP(bcast_d, 0, [[0, 128], [1, 1]])).then_inc(SM["qBC0"], 16)
            for k in range(24):
                r = k % 2
                if k >= 2:
                    g.wait_ge(SM["qDQ"], k - 1)
                w, og, kc = cseq[k]
                g.wait_ge(SM["qTD"], 16 * 2 * (og + 1))
                qrb = "qRB0" if r == 0 else "qRB1"
                g.dma_start(out=rsb[:, r * 1024:(r + 1) * 1024],
                            in_=scl_in_ap(w, og, kc, 0)).then_inc(SM[qrb], 16)
                g.dma_start(out=ssb[:, r * 1024:(r + 1) * 1024],
                            in_=scl_in_ap(w, og, kc, 1)).then_inc(SM[qrb], 16)
                for q in range(4):
                    kk = (k - 1) * 4 + q
                    if kk >= 0:
                        g.wait_ge(SM["qEVg"], kk + 1)
                        og2, tc = kk // 4, kk % 4
                        g.dma_start(
                            out=bass.AP(gate_d, (tc * IG + og2) * TILE, [[TCH, 128], [1, TCH]]),
                            in_=stg[:, (kk % 4) * TCH:(kk % 4 + 1) * TCH]).then_inc(SM["qS%d" % (kk % 4)], 16)
            for q in range(4):
                kk = 23 * 4 + q
                g.wait_ge(SM["qEVg"], kk + 1)
                og2, tc = kk // 4, kk % 4
                g.dma_start(
                    out=bass.AP(gate_d, (tc * IG + og2) * TILE, [[TCH, 128], [1, TCH]]),
                    in_=stg[:, (kk % 4) * TCH:(kk % 4 + 1) * TCH]).then_inc(SM["qS%d" % (kk % 4)], 16)
            g.wait_ge(SM["qMXg"], 3)
            g.dma_start(out=bass.AP(ccg_i, 0, [[1, 1], [1, 128]]),
                        in_=mxt[:, MX_G:MX_G + 1]).then_inc(SM["qBg"], 16)
            g.wait_ge(SM["qBg"], 16)
            g.collective_compute("AllReduce", ALU.max, replica_groups=[list(range(8))],
                                 ins=[ccg_i.ap().opt()], outs=[ccg_o.ap().opt()]).then_inc(SM["qCg"], 1)
            g.wait_ge(SM["qCg"], 1)
            g.dma_start(out=mrow[0:1, 128:256], in_=ccg_o.ap()).then_inc(SM["qBg"], 16)
            g.wait_ge(SM["qMXg"], 4)
            g.dma_start(out=bass.AP(bcast_d, 1, [[1, 1], [1, 2]]),
                        in_=sc1[0:1, 5:7]).then_inc(SM["qBg"], 16)
            g.wait_ge(SM["qBg"], 48)
            g.dma_start(out=bc[:, BC_CG:BC_CG + 1],
                        in_=bass.AP(bcast_d, 1, [[0, 128], [1, 1]])).then_inc(SM["qBCg"], 16)
            g.dma_start(out=bc[:, BC_S3:BC_S3 + 1],
                        in_=bass.AP(bcast_d, 2, [[0, 128], [1, 1]])).then_inc(SM["qBCg"], 16)
            for og in range(24):
                for pk in (24 + 2 * og, 25 + 2 * og):
                    r = pk % 2
                    if pk >= 2:
                        g.wait_ge(SM["qDQ"], pk - 1)
                    w, o2, kc = cseq[pk]
                    if w == "up":
                        g.wait_ge(SM["qTD"], 16 * (48 + 2 * (o2 + 1)))
                    else:
                        g.wait_ge(SM["qTD"], 16 * (96 + 2 * (o2 + 1)))
                    qrb = "qRB0" if r == 0 else "qRB1"
                    g.dma_start(out=rsb[:, r * 1024:(r + 1) * 1024],
                                in_=scl_in_ap(w, o2, kc, 0)).then_inc(SM[qrb], 16)
                    g.dma_start(out=ssb[:, r * 1024:(r + 1) * 1024],
                                in_=scl_in_ap(w, o2, kc, 1)).then_inc(SM[qrb], 16)
                for tc in range(4):
                    kk = og * 4 + tc
                    g.wait_ge(SM["qEVu"], kk + 1)
                    g.dma_start(
                        out=bass.AP(up_d, (tc * IG + og) * TILE, [[TCH, 128], [1, TCH]]),
                        in_=stg[:, (kk % 4) * TCH:(kk % 4 + 1) * TCH]).then_inc(SM["qS%d" % (kk % 4)], 16)
                for q in range(2):
                    kk = og * 2 + q
                    g.wait_ge(SM["qCHg"], kk + 1)
                    tck, i2k = kk // 12, kk % 12
                    g.dma_start(
                        out=bass.AP(p_d, (tck * IG + 2 * i2k) * TILE,
                                    [[TCH, 128], [TILE, 2], [1, TCH]]),
                        in_=ppb[:, (kk % 2) * 1024:(kk % 2 + 1) * 1024]
                        ).then_inc(SM["qSP20" if kk % 2 == 0 else "qSP21"], 16)
                dpos = 25 + 2 * og
                d = og
                o3, kc3 = d // 3, d % 3
                g.wait_ge(SM["qDQ"], dpos + 1)
                g.dma_start(
                    out=bass.AP(wqd_hi, (kc3 * 8 * G8 + o3) * 16384,
                                [[128, 128], [G8 * 16384, 8], [1, 128]]),
                    in_=wqh[:, (dpos % 2) * 1024:(dpos % 2 + 1) * 1024]).then_inc(SM["qWQS"], 16)
                g.dma_start(
                    out=bass.AP(wqd_lo, (kc3 * 8 * G8 + o3) * 16384,
                                [[128, 128], [G8 * 16384, 8], [1, 128]]),
                    in_=wql[:, (dpos % 2) * 1024:(dpos % 2 + 1) * 1024]).then_inc(SM["qWQS"], 16)
            g.wait_ge(SM["qMXu"], 1)
            g.dma_start(out=bass.AP(ccu_i, 0, [[1, 128], [128, 2]]),
                        in_=mxt[:, MX_UP:MX_UP + 2],
                        allow_slow_non_contiguous=True).then_inc(SM["qBu"], 16)
            g.wait_ge(SM["qBu"], 16)
            g.collective_compute("AllReduce", ALU.max, replica_groups=[list(range(8))],
                                 ins=[ccu_i.ap().opt()], outs=[ccu_o.ap().opt()]).then_inc(SM["qCu"], 1)
            g.wait_ge(SM["qCu"], 1)
            g.dma_start(out=mrow[0:1, 0:256], in_=ccu_o.ap()).then_inc(SM["qBu"], 16)
            g.wait_ge(SM["qMXu"], 2)
            g.dma_start(out=bass.AP(bcast_d, 3, [[1, 1], [1, 2]]),
                        in_=sc1[0:1, 11:13]).then_inc(SM["qBu"], 16)
            g.wait_ge(SM["qBu"], 48)
            g.dma_start(out=bc[:, BC_CUP:BC_CUP + 1],
                        in_=bass.AP(bcast_d, 3, [[0, 128], [1, 1]])).then_inc(SM["qBCu"], 16)
            g.dma_start(out=bc[:, BC_C2:BC_C2 + 1],
                        in_=bass.AP(bcast_d, 4, [[0, 128], [1, 1]])).then_inc(SM["qBCu"], 16)

        @blk.sync
        def _(sy):
            sy.wait_ge(SM["qXLS"], 16 * G8)
            for g in range(G8):
                if g >= 2:
                    sy.wait_ge(SM["qNX"], g - 1)
                sy.dma_start(out=xlb[:, (g % 2) * TH:(g % 2 + 1) * TH],
                             in_=bass.AP(xln_d, g * 128 * TH, [[TH, 128], [1, TH]])
                             ).then_inc(SM["qLxl0" if g % 2 == 0 else "qLxl1"], 16)
            for k in range(24):
                r = k % 2
                if k >= 2:
                    sy.wait_ge(SM["qDQ"], k - 1)
                sy.dma_start(out=wqf[:, r * 1024:(r + 1) * 1024],
                             in_=wt_in_ap(*cseq[k])).then_inc(SM["qWT0" if r == 0 else "qWT1"], 16)
            for og in range(24):
                for pk in (24 + 2 * og, 25 + 2 * og):
                    r = pk % 2
                    sy.wait_ge(SM["qDQ"], pk - 1)
                    sy.dma_start(out=wqf[:, r * 1024:(r + 1) * 1024],
                                 in_=wt_in_ap(*cseq[pk])).then_inc(SM["qWT0" if r == 0 else "qWT1"], 16)
                for q in range(2):
                    kk = og * 2 + q
                    rr = kk % 2
                    if kk == 0:
                        for j4 in range(4):
                            sy.wait_ge(SM["qS%d" % j4], 16 * 24)
                    if kk >= 2:
                        sy.wait_ge(SM["qCHa"], kk - 1)
                    tck, i2k = kk // 12, kk % 12
                    sy.dma_start(out=gbuf[:, rr * 1024:(rr + 1) * 1024],
                                 in_=bass.AP(gate_d, (tck * IG + 2 * i2k) * TILE,
                                             [[TCH, 128], [TILE, 2], [1, TCH]])
                                 ).then_inc(SM["qLg0" if rr == 0 else "qLg1"], 16)

        @blk.vector
        def _(v):
            v = Chain(v, "vself")
            v.memset(mxt[:, MX_G:MX_G + 1], 0.0)
            v.memset(mxt[:, MX_UP:MX_UP + 2], 0.0).then_inc("qMXg", 1)
            v.wait_ge(SM["qB0"], 32)
            v.reduce_max(sc1[0:1, 1:2], mrow[0:1, 0:128], axis=AX)
            v.tensor_scalar(sc1[0:1, 14:15], sc1[0:1, 1:2], 1.0 / QMAX, 1e-12, ALU.mult, ALU.max)
            v.reciprocal(sc1[0:1, 2:3], sc1[0:1, 14:15]).then_inc("qMXg", 1)
            v.wait_ge(SM["qBC0"], 16)
            for g in range(G8):
                r = g % 2
                v.wait_ge(SM["qLxl0" if r == 0 else "qLxl1"], 16 * (g // 2 + 1))
                xs = xlb[:, r * TH:(r + 1) * TH]
                v.tensor_scalar(nxf[:, :], xs, bc[:, BC_R1:BC_R1 + 1], MAGIC, ALU.mult, ALU.add)
                v.tensor_scalar(nxf[:, :], nxf[:, :], MAGIC, None, ALU.subtract)
                v.tensor_scalar(xs, nxf[:, :], 1.0 / 256.0, MAGIC, ALU.mult, ALU.add)
                v.tensor_scalar(nxa[:, g * TH:(g + 1) * TH], xs, MAGIC, 256.0,
                                ALU.subtract, ALU.mult)
                v.scalar_tensor_tensor(nxb[:, g * TH:(g + 1) * TH], nxa[:, g * TH:(g + 1) * TH],
                                       -1.0, nxf[:, :], ALU.mult, ALU.add).then_inc("qNX", 1)
            for k in range(24):
                r = k % 2
                ra, rb = r * 1024, (r + 1) * 1024
                v.wait_ge(SM["qWT0" if r == 0 else "qWT1"], 16 * (k // 2 + 1))
                v.wait_ge(SM["qRB0" if r == 0 else "qRB1"], 32 * (k // 2 + 1))
                v.tensor_mul(wqv[:, ra:rb], wqf[:, ra:rb], rsb[:, ra:rb])
                v.tensor_scalar(wqv[:, ra:rb], wqv[:, ra:rb], MAGIC, MAGIC, ALU.add, ALU.subtract)
                v.tensor_mul(wqv[:, ra:rb], wqv[:, ra:rb], ssb[:, ra:rb]).then_inc("qDN", 1)
                v.wait_ge(SM["qEV2"], k + 1)
                v.tensor_sub(wql[:, ra:rb], wqv[:, ra:rb], wqh[:, ra:rb]).then_inc("qDQ", 1)
                for q in range(4):
                    kk = k * 4 + q
                    v.wait_ge(SM["qMMg"], kk + 1)
                    v.wait_ge(SM["qEVg"], kk + 1)
                    v.reduce_max(tmx2[:, :], pb[kk % 2][:, :], axis=AX, apply_absolute_value=True)
                    v.tensor_max(mxt[:, MX_G:MX_G + 1], mxt[:, MX_G:MX_G + 1],
                                 tmx2[:, :]).then_inc("qMXt", 1)
            v.engine_nop().then_inc("qMXg", 1)
            v.wait_ge(SM["qBg"], 32)
            v.reduce_max(sc1[0:1, 4:5], mrow[0:1, 128:256], axis=AX)
            v.tensor_mul(sc1[0:1, 4:5], sc1[0:1, 4:5], sc1[0:1, 14:15])
            v.tensor_scalar(sc1[0:1, 15:16], sc1[0:1, 4:5], 1.0 / QMAX, 1e-12, ALU.mult, ALU.max)
            v.reciprocal(sc1[0:1, 5:6], sc1[0:1, 15:16])
            v.tensor_mul(sc1[0:1, 5:6], sc1[0:1, 5:6], sc1[0:1, 14:15])
            v.tensor_scalar(sc1[0:1, 6:7], sc1[0:1, 15:16], -1.0, None, ALU.mult).then_inc("qMXg", 1)
            v.tensor_scalar(sc1[0:1, 13:14], sc1[0:1, 15:16], 1.0 / 65536.0, None, ALU.mult)
            v.wait_ge(SM["qBCg"], 32)
            for og in range(24):
                for pk in (24 + 2 * og, 25 + 2 * og):
                    r = pk % 2
                    ra, rb = r * 1024, (r + 1) * 1024
                    v.wait_ge(SM["qWT0" if r == 0 else "qWT1"], 16 * (pk // 2 + 1))
                    v.wait_ge(SM["qRB0" if r == 0 else "qRB1"], 32 * (pk // 2 + 1))
                    v.tensor_mul(wqv[:, ra:rb], wqf[:, ra:rb], rsb[:, ra:rb])
                    v.tensor_scalar(wqv[:, ra:rb], wqv[:, ra:rb], MAGIC, MAGIC, ALU.add, ALU.subtract)
                    v.tensor_mul(wqv[:, ra:rb], wqv[:, ra:rb], ssb[:, ra:rb]).then_inc("qDN", 1)
                    v.wait_ge(SM["qEV2"], pk + 1)
                    v.tensor_sub(wql[:, ra:rb], wqv[:, ra:rb], wqh[:, ra:rb]).then_inc("qDQ", 1)
                for q in range(2):
                    kk = og * 2 + q
                    rr = kk % 2
                    ra, rb = rr * 1024, (rr + 1) * 1024
                    v.wait_ge(SM["qLg0" if rr == 0 else "qLg1"], 16 * (kk // 2 + 1))
                    v.tensor_scalar(ngb[:, ra:rb], gbuf[:, ra:rb], bc[:, BC_CG:BC_CG + 1],
                                    MAGIC, ALU.mult, ALU.add)
                    v.tensor_scalar(ngb[:, ra:rb], ngb[:, ra:rb], MAGIC, None,
                                    ALU.subtract).then_inc("qCHa", 1)
                    v.wait_ge(SM["qSXg"], kk + 1)
                    v.tensor_scalar_add(egb[:, :], egb[:, :], 1.0)
                    v.reciprocal(egb[:, :], egb[:, :])
                    v.tensor_scalar(egb[:, :], egb[:, :], 65536.0, MAGIC, ALU.mult, ALU.add)
                    v.tensor_scalar(egb[:, :], egb[:, :], MAGIC, 65535.0, ALU.subtract, ALU.min)
                    if kk >= 2:
                        v.wait_ge(SM["qSP20" if kk % 2 == 0 else "qSP21"], 16 * (kk // 2))
                    v.tensor_mul(ppb[:, ra:rb], ngb[:, ra:rb], egb[:, :])
                    v.reduce_max(tmx2[:, :], ppb[:, ra:rb], axis=AX, apply_absolute_value=True)
                    v.tensor_max(mxt[:, MX_P:MX_P + 1], mxt[:, MX_P:MX_P + 1],
                                 tmx2[:, :]).then_inc("qCHg", 1)
                for q in range(4):
                    kk = og * 4 + q
                    v.wait_ge(SM["qMMu"], kk + 1)
                    v.wait_ge(SM["qEVu"], kk + 1)
                    v.reduce_max(tmx2[:, :], pb[kk % 2][:, :], axis=AX, apply_absolute_value=True)
                    v.tensor_max(mxt[:, MX_UP:MX_UP + 1], mxt[:, MX_UP:MX_UP + 1],
                                 tmx2[:, :]).then_inc("qMXt2", 1)
            v.engine_nop().then_inc("qMXu", 1)
            v.wait_ge(SM["qBu"], 32)
            v.reduce_max(sc1[0:1, 9:10], mrow[0:1, 0:128], axis=AX)
            v.reduce_max(sc1[0:1, 10:11], mrow[0:1, 128:256], axis=AX)
            v.tensor_mul(sc1[0:1, 9:10], sc1[0:1, 9:10], sc1[0:1, 14:15])
            v.tensor_scalar(sc1[0:1, 7:8], sc1[0:1, 9:10], 1.0 / QMAX, 1e-12, ALU.mult, ALU.max)
            v.reciprocal(sc1[0:1, 11:12], sc1[0:1, 7:8])
            v.tensor_mul(sc1[0:1, 11:12], sc1[0:1, 11:12], sc1[0:1, 14:15])
            v.tensor_mul(sc1[0:1, 10:11], sc1[0:1, 10:11], sc1[0:1, 13:14])
            v.tensor_scalar(sc1[0:1, 8:9], sc1[0:1, 10:11], 1.0 / QMAX, 1e-12, ALU.mult, ALU.max)
            v.reciprocal(sc1[0:1, 12:13], sc1[0:1, 8:9])
            v.tensor_mul(sc1[0:1, 12:13], sc1[0:1, 12:13], sc1[0:1, 13:14])
            v.tensor_mul(sc1[0:1, 13:14], sc1[0:1, 8:9], sc1[0:1, 7:8]).then_inc("qMXu", 1)

        @blk.scalar
        def _(s):
            s = Chain(s, "sself")
            for k in range(24):
                r = k % 2
                ra, rb = r * 1024, (r + 1) * 1024
                for q in range(4):
                    kk = (k - 1) * 4 + q
                    if kk >= 0:
                        s.wait_ge(SM["qMMg"], kk + 1)
                        jj = kk % 4
                        if kk >= 4:
                            s.wait_ge(SM["qS%d" % jj], 16 * (kk // 4))
                        s.copy(stg[:, jj * TCH:(jj + 1) * TCH], pb[kk % 2][:, :]).then_inc("qEVg", 1)
                s.wait_ge(SM["qDN"], k + 1)
                if k >= 2:
                    s.wait_ge(SM["qMMg"], 4 * (k - 1))
                s.copy(wqh[:, ra:rb], wqv[:, ra:rb]).then_inc("qEV2", 1)
            for q in range(4):
                kk = 23 * 4 + q
                s.wait_ge(SM["qMMg"], kk + 1)
                jj = kk % 4
                s.wait_ge(SM["qS%d" % jj], 16 * (kk // 4))
                s.copy(stg[:, jj * TCH:(jj + 1) * TCH], pb[kk % 2][:, :]).then_inc("qEVg", 1)
            for og in range(24):
                for q in range(4):
                    kk = (og - 1) * 4 + q
                    if kk >= 0:
                        s.wait_ge(SM["qMMu"], kk + 1)
                        jj = kk % 4
                        s.wait_ge(SM["qS%d" % jj], 16 * (24 + kk // 4))
                        s.copy(stg[:, jj * TCH:(jj + 1) * TCH], pb[kk % 2][:, :]).then_inc("qEVu", 1)
                for pk in (24 + 2 * og, 25 + 2 * og):
                    r = pk % 2
                    ra, rb = r * 1024, (r + 1) * 1024
                    s.wait_ge(SM["qDN"], pk + 1)
                    prev = pk - 2
                    if prev >= 24:
                        if cseq[prev][0] == "up":
                            s.wait_ge(SM["qMMu"], 4 * ((prev - 24) // 2 + 1))
                        else:
                            s.wait_ge(SM["qWQS"], 32 * ((prev - 25) // 2 + 1))
                    elif prev >= 0:
                        s.wait_ge(SM["qMMg"], 4 * 24)
                    s.copy(wqh[:, ra:rb], wqv[:, ra:rb]).then_inc("qEV2", 1)
                for q in range(2):
                    kk = og * 2 + q
                    rr = kk % 2
                    s.wait_ge(SM["qCHa"], kk + 1)
                    if kk >= 1:
                        s.wait_ge(SM["qCHg"], kk)
                    s.wait_ge(SM["qBCg"], 32)
                    s.activation(egb[:, :], ngb[:, rr * 1024:(rr + 1) * 1024], AF.Exp,
                                 scale=bc[:, BC_S3:BC_S3 + 1]).then_inc("qSXg", 1)
            for q in range(4):
                kk = 23 * 4 + q
                s.wait_ge(SM["qMMu"], kk + 1)
                jj = kk % 4
                s.wait_ge(SM["qS%d" % jj], 16 * (24 + kk // 4))
                s.copy(stg[:, jj * TCH:(jj + 1) * TCH], pb[kk % 2][:, :]).then_inc("qEVu", 1)

        @blk.tensor
        def _(t):
            t.wait_ge(SM["qNX"], G8)
            for ph, qmm in ((0, "qMMg"), (1, "qMMu")):
                for og in range(24):
                    pos = mm_piece_pos(ph, og)
                    r = pos % 2
                    t.wait_ge(SM["qDQ"], pos + 1)
                    for tc in range(4):
                        kk = og * 4 + tc
                        f = ph * 96 + kk
                        if f >= 2:
                            p = f - 2
                            if p < 96:
                                t.wait_ge(SM["qEVg"], p + 1)
                                t.wait_ge(SM["qMXt"], p + 1)
                            else:
                                t.wait_ge(SM["qEVu"], p - 96 + 1)
                                t.wait_ge(SM["qMXt2"], p - 96 + 1)
                        bank = pb[kk % 2]
                        for kg in range(G8):
                            lh = wqh[:, r * 1024 + kg * 128:r * 1024 + (kg + 1) * 128]
                            ll = wql[:, r * 1024 + kg * 128:r * 1024 + (kg + 1) * 128]
                            rh = nxa[:, kg * TH + tc * TCH:kg * TH + (tc + 1) * TCH]
                            rl = nxb[:, kg * TH + tc * TCH:kg * TH + (tc + 1) * TCH]
                            t.matmul(bank[:, :], lh, rh, start=(kg == 0), stop=False)
                            t.matmul(bank[:, :], lh, rl, start=False, stop=False)
                            mm = t.matmul(bank[:, :], ll, rh, start=False, stop=(kg == G8 - 1))
                            if kg == G8 - 1:
                                mm.then_inc(SM[qmm], 1)
    esC1.close()

    esC2 = ExitStack()
    ub = esC2.enter_context(nc.sbuf_tensor("ub", [128, 2 * 1024], F32))
    pbf = esC2.enter_context(nc.sbuf_tensor("pbf", [128, 2 * 1024], F32))
    nupb = esC2.enter_context(nc.sbuf_tensor("nupb", [128, 1024], F32))
    p2b = esC2.enter_context(nc.sbuf_tensor("p2b", [128, 2 * 1024], F32))
    cstP = esC2.enter_context(nc.sbuf_tensor("cstP", [128, 4], F32))
    tmx3 = esC2.enter_context(nc.sbuf_tensor("tmx3", [128, 1], F32))
    MX_P2 = MX_P + 1

    if True:
        blk = _Collect(SECT)
        @blk.sync
        def _(sy):
            for j4 in range(4):
                sy.wait_ge(SM["qS%d" % j4], 16 * 48)
            sy.wait_ge(SM["qSP20"], 16 * 24)
            sy.wait_ge(SM["qSP21"], 16 * 24)
            for k in range(48):
                r = k % 2
                tc, i2 = k // 12, k % 12
                if k >= 2:
                    sy.wait_ge(SM["qV2"], k - 1)
                sy.dma_start(out=ub[:, r * 1024:(r + 1) * 1024],
                             in_=bass.AP(up_d, (tc * IG + 2 * i2) * TILE,
                                         [[TCH, 128], [TILE, 2], [1, TCH]])
                             ).then_inc(SM["qLu0" if r == 0 else "qLu1"], 16)
                sy.dma_start(out=pbf[:, r * 1024:(r + 1) * 1024],
                             in_=bass.AP(p_d, (tc * IG + 2 * i2) * TILE,
                                         [[TCH, 128], [TILE, 2], [1, TCH]])
                             ).then_inc(SM["qLp0" if r == 0 else "qLp1"], 16)
                if k >= 1:
                    kk = k - 1
                    sy.wait_ge(SM["qV2"], kk + 1)
                    tck, i2k = kk // 12, kk % 12
                    sy.dma_start(
                        out=bass.AP(p2_d, (tck * IG + 2 * i2k) * TILE,
                                    [[TCH, 128], [TILE, 2], [1, TCH]]),
                        in_=p2b[:, (kk % 2) * 1024:(kk % 2 + 1) * 1024]
                        ).then_inc(SM["qP2a" if kk % 2 == 0 else "qP2b"], 16)
            kk = 47
            sy.wait_ge(SM["qV2"], 48)
            sy.dma_start(
                out=bass.AP(p2_d, ((kk // 12) * IG + 2 * (kk % 12)) * TILE,
                            [[TCH, 128], [TILE, 2], [1, TCH]]),
                in_=p2b[:, (kk % 2) * 1024:(kk % 2 + 1) * 1024]
                ).then_inc(SM["qP2a" if kk % 2 == 0 else "qP2b"], 16)

        @blk.scalar
        def _(s):
            s = Chain(s, "sself")
            for k in range(48):
                r = k % 2
                s.wait_ge(SM["qLu0" if r == 0 else "qLu1"], 16 * (k // 2 + 1))
                s.wait_ge(SM["qBCu"], 32)
                s.wait_ge(SM["qCNv"], 2)
                if k >= 1:
                    s.wait_ge(SM["qV2"], k)
                s.activation(nupb[:, :], ub[:, r * 1024:(r + 1) * 1024], AF.Identity,
                             scale=bc[:, BC_CUP:BC_CUP + 1], bias=cstP[:, 0:1])
                s.activation(nupb[:, :], nupb[:, :], AF.Identity,
                             scale=cstP[:, 2:3], bias=cstP[:, 1:2]).then_inc("qNUP", 1)

        @blk.vector
        def _(v):
            v = Chain(v, "vself")
            v.wait_ge(SM["qMMu"], 96)
            v.memset(cstP[:, 0:1], MAGIC)
            v.memset(cstP[:, 1:2], -MAGIC)
            v.memset(cstP[:, 2:3], 1.0)
            v.memset(mxt[:, MX_P2:MX_P2 + 1], 0.0).then_inc("qCNv", 1)
            for k in range(48):
                r = k % 2
                ra, rb = r * 1024, (r + 1) * 1024
                v.wait_ge(SM["qLp0" if r == 0 else "qLp1"], 16 * (k // 2 + 1))
                v.wait_ge(SM["qBCu"], 32)
                v.tensor_scalar(pbf[:, ra:rb], pbf[:, ra:rb], bc[:, BC_C2:BC_C2 + 1],
                                MAGIC, ALU.mult, ALU.add)
                v.tensor_scalar(pbf[:, ra:rb], pbf[:, ra:rb], MAGIC, None, ALU.subtract)
                v.wait_ge(SM["qNUP"], k + 1)
                if k >= 2:
                    v.wait_ge(SM["qP2a" if k % 2 == 0 else "qP2b"], 16 * (k // 2))
                v.tensor_mul(p2b[:, ra:rb], pbf[:, ra:rb], nupb[:, :])
                v.reduce_max(tmx3[:, :], p2b[:, ra:rb], axis=AX, apply_absolute_value=True)
                v.tensor_max(mxt[:, MX_P2:MX_P2 + 1], mxt[:, MX_P2:MX_P2 + 1],
                             tmx3[:, :]).then_inc("qV2", 1)
            v.engine_nop().then_inc("qMX2", 1)
            v.wait_ge(SM["qB3"], 32)
            v.reduce_max(sc1[0:1, 10:11], mrow[0:1, 0:128], axis=AX)
            v.tensor_mul(sc1[0:1, 10:11], sc1[0:1, 10:11], sc1[0:1, 13:14])
            v.tensor_scalar(sc1[0:1, 9:10], sc1[0:1, 10:11], 1.0 / QMAX, 1e-12,
                            ALU.mult, ALU.max)
            v.reciprocal(sc1[0:1, 12:13], sc1[0:1, 9:10])
            v.tensor_mul(sc1[0:1, 12:13], sc1[0:1, 12:13], sc1[0:1, 13:14]).then_inc("qMX2", 1)

        @blk.gpsimd
        def _(g):
            g.wait_ge(SM["qMX2"], 1)
            g.dma_start(out=bass.AP(cc3_i, 0, [[1, 1], [1, 128]]),
                        in_=mxt[:, MX_P2:MX_P2 + 1]).then_inc(SM["qB3"], 16)
            g.wait_ge(SM["qB3"], 16)
            g.collective_compute("AllReduce", ALU.max, replica_groups=[list(range(8))],
                                 ins=[cc3_i.ap().opt()], outs=[cc3_o.ap().opt()]).then_inc(SM["qC3"], 1)
            g.wait_ge(SM["qC3"], 1)
            g.dma_start(out=mrow[0:1, 0:128], in_=cc3_o.ap()).then_inc(SM["qB3"], 16)
            g.wait_ge(SM["qMX2"], 2)
            g.dma_start(out=bass.AP(bcast_d, 5, [[1, 1], [1, 1]]),
                        in_=sc1[0:1, 9:10]).then_inc(SM["qB3"], 16)
            g.dma_start(out=bass.AP(bcast_d, 6, [[1, 1], [1, 1]]),
                        in_=sc1[0:1, 12:13]).then_inc(SM["qB3"], 16)
            g.wait_ge(SM["qB3"], 64)
            g.dma_start(out=bc[:, BC_S5:BC_S5 + 1],
                        in_=bass.AP(bcast_d, 5, [[0, 128], [1, 1]])).then_inc(SM["qBC3"], 16)
            g.dma_start(out=bc[:, BC_C3:BC_C3 + 1],
                        in_=bass.AP(bcast_d, 6, [[0, 128], [1, 1]])).then_inc(SM["qBC3"], 16)
    esC2.close()

    esC3 = ExitStack()
    dnA = esC3.enter_context(nc.sbuf_tensor("dnA", [128, 2 * IG * TCH], BF16))
    dnB = esC3.enter_context(nc.sbuf_tensor("dnB", [128, 2 * IG * TCH], BF16))
    wdh = esC3.enter_context(nc.sbuf_tensor("wdh", [128, 2 * 3072], BF16))
    wdl = esC3.enter_context(nc.sbuf_tensor("wdl", [128, 2 * 3072], BF16))
    p2r = esC3.enter_context(nc.sbuf_tensor("p2r", [128, 2 * 1024], F32))
    nob = esC3.enter_context(nc.sbuf_tensor("nob", [128, 1024], F32))
    xrb = esC3.enter_context(nc.sbuf_tensor("xrb", [128, G8 * TCH], F32))
    h2b = esC3.enter_context(nc.sbuf_tensor("h2b", [128, G8 * TCH], F32))
    sqt2 = esC3.enter_context(nc.sbuf_tensor("sqt2", [128, TCH], F32))
    cstQ = esC3.enter_context(nc.sbuf_tensor("cstQ", [128, 4], F32))

    if True:
        blk = _Collect(SECT)
        @blk.sync
        def _(sy):
            sy.wait_ge(SM["qP2a"], 16 * 24)
            sy.wait_ge(SM["qP2b"], 16 * 24)
            sy.wait_ge(SM["qWQS"], 16 * 48)
            sy.wait_ge(SM["qNUP"], 48)
            for tc in range(NTC):
                if tc >= 1:
                    sy.wait_ge(SM["qOE2"], tc * G8)
                sy.dma_start(out=xrb[:, 0:4096],
                             in_=bass.AP(xln_d, tc * TCH,
                                         [[TH, 128], [128 * TH, G8], [1, TCH]])
                             ).then_inc(SM["qLx20"], 16)
                for i2 in range(12):
                    k = tc * 12 + i2
                    r = k % 2
                    if k >= 2:
                        sy.wait_ge(SM["qNO"], k - 1)
                    sy.dma_start(out=p2r[:, r * 1024:(r + 1) * 1024],
                                 in_=bass.AP(p2_d, (tc * IG + 2 * i2) * TILE,
                                             [[TCH, 128], [TILE, 2], [1, TCH]])
                                 ).then_inc(SM["qLq0" if r == 0 else "qLq1"], 16)
                for og in range(G8):
                    m = tc * G8 + og
                    r = m % 2
                    if m >= 2:
                        sy.wait_ge(SM["qMM2"], m - 1)
                    sy.dma_start(out=wdh[:, r * 3072:(r + 1) * 3072],
                                 in_=bass.AP(wqd_hi, og * 16384,
                                             [[128, 128], [G8 * 16384, IG], [1, 128]])
                                 ).then_inc(SM["qLW0" if r == 0 else "qLW1"], 16)
                    sy.dma_start(out=wdl[:, r * 3072:(r + 1) * 3072],
                                 in_=bass.AP(wqd_lo, og * 16384,
                                             [[128, 128], [G8 * 16384, IG], [1, 128]])
                                 ).then_inc(SM["qLW0" if r == 0 else "qLW1"], 16)
                sy.wait_ge(SM["qLAv"], 2 * NTC + 2 * tc + 2)
                for g in range(G8):
                    sy.dma_start(
                        out=bass.AP(out_fm.tensor, g * 128 * TH + tc * TCH,
                                    [[TH, 128], [1, TCH]]),
                        in_=h2b[:, g * TCH:(g + 1) * TCH]).then_inc(SM["qST"], 16)
            sy.wait_ge(SM["qST"], ST0 + 16 * 32)

        @blk.vector
        def _(v):
            v = Chain(v, "vself")
            v.wait_ge(SM["qNUP"], 48)
            v.memset(cstQ[:, 0:1], MAGIC)
            v.memset(cstQ[:, 1:2], -MAGIC).then_inc("qCNv", 1)
            for tc in range(NTC):
                rx = (tc % 2) * 4096
                for i2 in range(12):
                    k = tc * 12 + i2
                    r = k % 2
                    ra, rb = r * 1024, (r + 1) * 1024
                    v.wait_ge(SM["qLq0" if r == 0 else "qLq1"], 16 * (k // 2 + 1))
                    v.wait_ge(SM["qBC3"], 32)
                    if tc >= 2 and i2 == 0:
                        v.wait_ge(SM["qMM2"], (tc - 1) * G8)
                    v.tensor_scalar(nob[:, :], p2r[:, ra:rb], bc[:, BC_C3:BC_C3 + 1],
                                    MAGIC, ALU.mult, ALU.add)
                    v.tensor_scalar(nob[:, :], nob[:, :], MAGIC, None, ALU.subtract)
                    v.tensor_scalar(p2r[:, ra:rb], nob[:, :], 1.0 / 256.0, MAGIC,
                                    ALU.mult, ALU.add)
                    dst = (tc % 2) * IG * TCH + 2 * i2 * TCH
                    v.tensor_scalar(dnA[:, dst:dst + 1024], p2r[:, ra:rb], MAGIC, 256.0,
                                    ALU.subtract, ALU.mult)
                    v.scalar_tensor_tensor(dnB[:, dst:dst + 1024], dnA[:, dst:dst + 1024],
                                           -1.0, nob[:, :], ALU.mult, ALU.add).then_inc("qNO", 1)
                for og in range(G8):
                    m = tc * G8 + og
                    v.wait_ge(SM["qMM2"], m + 1)
                    v.wait_ge(SM["qBC3"], 32)
                    if tc >= 1 and og == 0:
                        v.wait_ge(SM["qST"], ST0 + 16 * G8 * tc)
                    v.wait_ge(SM["qLx20"], 16 * (tc + 1))
                    v.scalar_tensor_tensor(h2b[:, og * TCH:(og + 1) * TCH], pb[m % 2][:, :],
                                           bc[:, BC_S5:BC_S5 + 1], xrb[:, og * TCH:(og + 1) * TCH],
                                           ALU.mult, ALU.add).then_inc("qOE2", 1)
                v.wait_ge(SM["qSM"], G8 * (NTC + tc + 1))
                v.tensor_scalar_mul(lnt[0:1, 0:TCH], pb[4][0:1, :], 1.0 / C)
                v.tensor_scalar_mul(lnt[0:1, TCH:2 * TCH], pb[5][0:1, :], 1.0 / C)
                v.tensor_mul(lnt[0:1, 2 * TCH:3 * TCH], lnt[0:1, 0:TCH], lnt[0:1, 0:TCH])
                v.tensor_sub(lnt[0:1, TCH:2 * TCH], lnt[0:1, TCH:2 * TCH],
                             lnt[0:1, 2 * TCH:3 * TCH]).then_inc("qLAv", 1)
                v.wait_ge(SM["qLAs"], NTC + tc + 1)
                v.tensor_mul(lnt[0:1, 3 * TCH:4 * TCH], lnt[0:1, 0:TCH],
                             lnt[0:1, 2 * TCH:3 * TCH]).then_inc("qLB", 1)
                v.wait_ge(SM["qLS"], NTC + tc + 1)
                for g in range(G8):
                    xs = h2b[:, g * TCH:(g + 1) * TCH]
                    v.tensor_mul(xs, xs, pb[6][:, :])
                    v.tensor_sub(xs, xs, pb[7][:, :])
                v.engine_nop().then_inc("qLAv", 1)

        @blk.scalar
        def _(s):
            s = Chain(s, "sself")
            for tc in range(NTC):
                for g in range(G8):
                    idx = NTC * G8 + tc * G8 + g
                    s.wait_ge(SM["qOE2"], tc * G8 + g + 1)
                    if idx >= 1:
                        s.wait_ge(SM["qSM"], idx)
                    s.activation(sqt2[:, :], h2b[:, g * TCH:(g + 1) * TCH],
                                 AF.Square).then_inc("qSQ", 1)
                s.wait_ge(SM["qLAv"], 2 * NTC + 2 * tc + 1)
                s.activation(lnt[0:1, 2 * TCH:3 * TCH], lnt[0:1, TCH:2 * TCH], AF.Ln,
                             bias=cst[0:1, CS_EPS:CS_EPS + 1])
                s.activation(lnt[0:1, 2 * TCH:3 * TCH], lnt[0:1, 2 * TCH:3 * TCH], AF.Exp,
                             scale=cst[0:1, CS_NHALF:CS_NHALF + 1]).then_inc("qLAs", 1)

        @blk.tensor
        def _(t):
            for tc in range(NTC):
                for og in range(G8):
                    m = tc * G8 + og
                    r = m % 2
                    t.wait_ge(SM["qNO"], 12 * (tc + 1))
                    t.wait_ge(SM["qLW0" if r == 0 else "qLW1"], 32 * (m // 2 + 1))
                    if m >= 2:
                        t.wait_ge(SM["qOE2"], m - 1)
                    bank = pb[m % 2]
                    for kg in range(IG):
                        lh = wdh[:, r * 3072 + kg * 128:r * 3072 + (kg + 1) * 128]
                        ll = wdl[:, r * 3072 + kg * 128:r * 3072 + (kg + 1) * 128]
                        dsrc = (tc % 2) * IG * TCH + kg * TCH
                        rh = dnA[:, dsrc:dsrc + TCH]
                        rl = dnB[:, dsrc:dsrc + TCH]
                        t.matmul(bank[:, :], lh, rh, start=(kg == 0), stop=False)
                        t.matmul(bank[:, :], lh, rl, start=False, stop=False)
                        mm = t.matmul(bank[:, :], ll, rh, start=False, stop=(kg == IG - 1))
                        if kg == IG - 1:
                            mm.then_inc(SM["qMM2"], 1)
                for g in range(G8):
                    t.wait_ge(SM["qOE2"], tc * G8 + g + 1)
                    xs = h2b[:, g * TCH:(g + 1) * TCH]
                    t.matmul(pb[4][0:1, :], ones_col[:, :], xs,
                             start=(g == 0), stop=(g == G8 - 1))
                    t.wait_ge(SM["qSQ"], NTC * G8 + tc * G8 + g + 1)
                    t.matmul(pb[5][0:1, :], ones_col[:, :], sqt2[:, :],
                             start=(g == 0), stop=(g == G8 - 1)).then_inc(SM["qSM"], 1)
                t.wait_ge(SM["qLB"], NTC + tc + 1)
                t.matmul(pb[6][:, :], ones_row[:, :], lnt[0:1, 2 * TCH:3 * TCH],
                         start=True, stop=True)
                t.matmul(pb[7][:, :], ones_row[:, :], lnt[0:1, 3 * TCH:4 * TCH],
                         start=True, stop=True).then_inc(SM["qLS"], 1)

        @blk.gpsimd
        def _(g):
            g.wait_ge(SM["qST"], ST0 + 16 * 32)
    esC3.close()

    with nc.Block() as blk:
        @blk.sync
        def _(sy):
            for f in SECT.get("sync", []):
                f(sy)

        @blk.gpsimd
        def _(g):
            for f in SECT.get("gpsimd", []):
                f(g)

        @blk.vector
        def _(v):
            for f in SECT.get("vector", []):
                f(v)

        @blk.scalar
        def _(s):
            for f in SECT.get("scalar", []):
                f(s)

        @blk.tensor
        def _(t):
            for f in SECT.get("tensor", []):
                f(t)

    esB.close()
    es.close()
    return nc


def host_prep(L0_attn_x, L0_attn_W_rvk, L0_attn_W_out, L0_attn_time_decay,
              L0_attn_time_first, L0_ffn_w_gate, L0_ffn_w_up, L0_ffn_w_down):
    x = np.ascontiguousarray(np.asarray(L0_attn_x), dtype=np.float32)
    W_rvk = np.asarray(L0_attn_W_rvk)
    W_out = np.asarray(L0_attn_W_out)
    td = np.asarray(L0_attn_time_decay)
    tf = np.asarray(L0_attn_time_first)
    wg = np.ascontiguousarray(np.asarray(L0_ffn_w_gate), dtype=np.float32)
    wu = np.ascontiguousarray(np.asarray(L0_ffn_w_up), dtype=np.float32)
    wd = np.ascontiguousarray(np.asarray(L0_ffn_w_down), dtype=np.float32)
    woutT = np.ascontiguousarray(W_out.T, dtype=np.float32)
    ident = np.eye(128, dtype=np.float32)
    wupT = np.ascontiguousarray(wu.T)
    wgateT = np.ascontiguousarray(wg.T)
    wdownT = np.ascontiguousarray(wd.T)
    in_maps = []
    for c in range(8):
        b, h = c // 2, c % 2
        sel = np.r_[h * CH:(h + 1) * CH, C + h * CH:C + (h + 1) * CH,
                    2 * C + h * CH:2 * C + (h + 1) * CH]
        wrvkT = np.ascontiguousarray(W_rvk[sel, :].T, dtype=np.float32)
        xt = np.ascontiguousarray(x[b].T)
        xth = np.ascontiguousarray(xt[:, h * TH:(h + 1) * TH])
        tdv = np.ascontiguousarray(td[h * CH:(h + 1) * CH].reshape(CG, 128).T,
                                   dtype=np.float32)
        tfv = np.ascontiguousarray(tf[h * CH:(h + 1) * CH].reshape(CG, 128).T,
                                   dtype=np.float32)
        agidx = np.zeros((1, G8 * NTC), dtype=np.int32)
        for tc in range(NTC):
            agidx[0, tc] = (2 * b * CG * NCH + (4 * h + tc)) * 128 * 512
        in_maps.append({
            "xT": xt, "xTh": xth, "wrvkT": wrvkT, "woutT_d": woutT,
            "tdv": tdv, "tfv": tfv, "ident_in": ident, "agidx_in": agidx,
            "wup": wu, "wupT": wupT, "wgate": wg, "wgateT": wgateT,
            "wdown": wd, "wdownT": wdownT,
        })
    return in_maps


def kernel(L0_attn_x, L0_attn_W_rvk, L0_attn_W_out, L0_attn_time_decay,
           L0_attn_time_first, L0_ffn_w_gate, L0_ffn_w_up, L0_ffn_w_down):
    in_maps = host_prep(L0_attn_x, L0_attn_W_rvk, L0_attn_W_out, L0_attn_time_decay,
                        L0_attn_time_first, L0_ffn_w_gate, L0_ffn_w_up, L0_ffn_w_down)
    nc = build_nc(debug=0)
    res = run_bass_kernel_spmd(nc, in_maps, core_ids=list(range(8)))
    out = np.empty((B, T, C), dtype=np.float32)
    for c in range(8):
        b, h = c // 2, c % 2
        out[b, h * TH:(h + 1) * TH, :] = res.results[c]["out_fm"].T
    return out
